# version 14
# baseline (speedup 1.0000x reference)
import numpy as np
from contextlib import ExitStack

import concourse.bass as bass
import concourse.mybir as mybir
from concourse.bass_utils import run_bass_kernel_spmd

F32 = mybir.dt.float32
BF16 = mybir.dt.bfloat16
I32 = mybir.dt.int32
I16 = mybir.dt.int16
AF = mybir.ActivationFunctionType
ALU = mybir.AluOpType

NCORE = 8
B, S, DM = 2, 8192, 4096
ROWS = B * S
MROWS = ROWS // NCORE
NMT = MROWS // 128
IM = 11008
NNT = IM // 128
L = 4096
KT = DM // 128
BW = 8
BSTARTS = list(range(0, NNT, BW))
NB = len(BSTARTS)


class Sem:
    def __init__(self, handle):
        self.h = handle
        self.n = 0

    def inc(self, instr, k=1):
        instr.then_inc(self.h, k)
        self.n += k
        return instr


def host_consts():
    l = np.arange(L)
    Om = np.exp(-2j * np.pi * l / L)
    t_l = ((1 - Om) / (1 + Om)).imag
    c_l = 2.0 / (1 + Om)

    def to_pc(a):
        return np.ascontiguousarray(a.reshape(128, 32)).astype(np.float32)

    p0 = np.arange(128)
    t0g, cg = np.meshgrid(np.arange(128), np.arange(32), indexing="ij")
    CosM = np.cos(2 * np.pi * np.outer(p0, p0) / 128).astype(np.float32)
    SinM = np.sin(2 * np.pi * np.outer(p0, p0) / 128).astype(np.float32)
    c1 = np.arange(32)
    oneh = np.zeros((128, KT * KT), np.float32)
    for K in range(KT):
        oneh[:, KT * K + K] = 1.0
    return {
        "t_row": t_l.astype(np.float32).reshape(1, L),
        "cr_pc": to_pc(c_l.real), "ci_pc": to_pc(c_l.imag),
        "CosM": CosM, "SinM": SinM, "SinMneg": np.ascontiguousarray(-SinM),
        "Wtr": np.cos(2 * np.pi * cg * t0g / L).astype(np.float32),
        "Wti": np.sin(2 * np.pi * cg * t0g / L).astype(np.float32),
        "Er": (np.cos(2 * np.pi * np.outer(c1, c1) / 32) / L).astype(np.float32),
        "Ei": (np.sin(2 * np.pi * np.outer(c1, c1) / 32) / L).astype(np.float32),
        "Eineg": np.ascontiguousarray(-(np.sin(2 * np.pi * np.outer(c1, c1) / 32) / L)).astype(np.float32),
        "ident": np.eye(128, dtype=np.float32),
    }


def build():
    nc = bass.Bass("TRN2", target_bir_lowering=False)
    din = lambda n, s, d=F32: nc.dram_tensor(n, s, d, kind="ExternalInput").ap()

    x_in = din("x", [MROWS, DM])
    lam = din("lam", [64, 2]); p_ri = din("p_ri", [64, 2]); q_ri = din("q_ri", [64, 2])
    vc = din("vc", [64, 64, 2]); ct_in = din("ct_in", [1, 64, 2]); bp = din("bp", [64, 1])
    lsr = din("lsr", [64, 1]); dd_in = din("dd", [128, 1])
    gqw = din("gqw", [DM, IM // 8], I32); gsc = din("gsc", [32, IM]); gqz = din("gqz", [32, IM // 8], I32)
    uqw = din("uqw", [DM, IM // 8], I32); usc = din("usc", [32, IM]); uqz = din("uqz", [32, IM // 8], I32)
    dqw = din("dqw", [IM, DM // 8], I32); dsc = din("dsc", [86, DM]); dqz = din("dqz", [86, DM // 8], I32)
    t_row = din("t_row", [1, L])
    cr_pc = din("cr_pc", [128, 32]); ci_pc = din("ci_pc", [128, 32])
    CosM = din("CosM", [128, 128]); SinM = din("SinM", [128, 128]); SinMneg = din("SinMneg", [128, 128])
    Wtr = din("Wtr", [128, 32]); Wti = din("Wti", [128, 32])
    Er = din("Er", [32, 32]); Ei = din("Ei", [32, 32]); Eineg = din("Eineg", [32, 32])
    ident = din("ident", [128, 128])

    out = nc.dram_tensor("out", [MROWS, DM], F32, kind="ExternalOutput").ap()

    kscr = nc.dram_tensor("kscr", [8, L], F32, kind="Internal").ap()
    kpad = nc.dram_tensor("kpad", [1, 128 + L], BF16, kind="Internal").ap()
    ssc_g = nc.dram_tensor("ssc_g", [NNT, 32, 128], BF16, kind="Internal").ap()
    zsc_g = nc.dram_tensor("zsc_g", [NNT, 32, 128], BF16, kind="Internal").ap()
    ssc_u = nc.dram_tensor("ssc_u", [NNT, 32, 128], BF16, kind="Internal").ap()
    zsc_u = nc.dram_tensor("zsc_u", [NNT, 32, 128], BF16, kind="Internal").ap()
    ssc_d = nc.dram_tensor("ssc_d", [8, 86, 512], BF16, kind="Internal").ap()
    zsc_d = nc.dram_tensor("zsc_d", [8, 86, 512], BF16, kind="Internal").ap()
    hscr = nc.dram_tensor("hscr", [4, 128, NNT, 512], BF16, kind="Internal").ap()
    cscr = nc.dram_tensor("cscr", [MROWS, DM], F32, kind="Internal").ap()

    es0 = ExitStack()
    mk0 = es0.enter_context
    sb0 = lambda n, s, d=F32: mk0(nc.sbuf_tensor(n, s, d))

    LD = Sem(mk0(nc.semaphore("LD")))
    SLp = [Sem(mk0(nc.semaphore(f"SL{i}"))) for i in range(2)]
    DSS = [Sem(mk0(nc.semaphore(f"DSS{i}"))) for i in range(2)]
    XQp = [Sem(mk0(nc.semaphore(f"XQ{i}"))) for i in range(2)]
    XG = Sem(mk0(nc.semaphore("XG")))
    IQp = [Sem(mk0(nc.semaphore(f"IQ{i}"))) for i in range(2)]
    BCg = [Sem(mk0(nc.semaphore(f"BCg{i}"))) for i in range(4)]
    BCu = [Sem(mk0(nc.semaphore(f"BCu{i}"))) for i in range(4)]
    BCd = [Sem(mk0(nc.semaphore(f"BCd{i}"))) for i in range(2)]
    HSp = [Sem(mk0(nc.semaphore(f"HS{i}"))) for i in range(2)]
    CVS = Sem(mk0(nc.semaphore("CVS")))
    HLp = [Sem(mk0(nc.semaphore(f"HL{i}"))) for i in range(8)]
    SOp = [Sem(mk0(nc.semaphore(f"SO{i}"))) for i in range(2)]
    DS = Sem(mk0(nc.semaphore("DS")))
    TB = Sem(mk0(nc.semaphore("TB")))
    CV = Sem(mk0(nc.semaphore("CV")))
    CS = Sem(mk0(nc.semaphore("CS")))
    CT = Sem(mk0(nc.semaphore("CT")))

    s_dd = sb0("s_dd", [128, 1])
    s_rstd = sb0("s_rstd", [128, NMT])
    s_nbias = sb0("s_nbias", [128, NMT])
    s_id = sb0("s_id", [128, 128])
    s_idbf = sb0("s_idbf", [128, 128], BF16)

    es_xnT = ExitStack()
    s_xnT = es_xnT.enter_context(nc.sbuf_tensor("s_xnT", [128, KT * MROWS], BF16))

    es_T = ExitStack()
    s_T = es_T.enter_context(nc.sbuf_tensor("s_T", [128, 32 * 128], BF16))

    marks = {}

    with es0, nc.Block() as block:
        es1 = ExitStack()
        mk1 = es1.enter_context
        sb1 = lambda n, s, d=F32: mk1(nc.sbuf_tensor(n, s, d))

        s_lam = sb1("s_lam", [64, 2]); s_p = sb1("s_p", [64, 2]); s_q = sb1("s_q", [64, 2])
        s_vc = sb1("s_vc", [64, 128]); s_vct = sb1("s_vct", [64, 128])
        s_ct = sb1("s_ct", [64, 2]); s_bp = sb1("s_bp", [64, 1]); s_b = sb1("s_b", [64, 2])
        s_lsr = sb1("s_lsr", [64, 1]); s_step = sb1("s_step", [64, 1]); s_s2 = sb1("s_s2", [64, 1])
        s_lr2 = sb1("s_lr2", [64, 1]); s_lrneg = sb1("s_lrneg", [64, 1])
        s_vP = sb1("s_vP", [64, 8]); s_vQ = sb1("s_vQ", [64, 8]); s_vt = sb1("s_vt", [64, 8])
        s_cr = sb1("s_cr", [128, 32]); s_ci = sb1("s_ci", [128, 32])
        s_Wtr = sb1("s_Wtr", [128, 32]); s_Wti = sb1("s_Wti", [128, 32])
        s_CosM = sb1("s_CosM", [128, 128]); s_SinM = sb1("s_SinM", [128, 128])
        s_SinMneg = sb1("s_SinMneg", [128, 128])
        s_Er = sb1("s_Er", [32, 32]); s_Ei = sb1("s_Ei", [32, 32]); s_Eineg = sb1("s_Eineg", [32, 32])
        kk = [sb1(f"kk{i}", [128, 32]) for i in range(8)]
        cw = [sb1(f"cw{i}", [128, 32]) for i in range(10)]
        s_atr = sb1("s_atr", [128, 32]); s_ati = sb1("s_ati", [128, 32])
        s_Fr = sb1("s_Fr", [128, 32]); s_Fi = sb1("s_Fi", [128, 32])
        s_Gr = sb1("s_Gr", [128, 32]); s_Gi = sb1("s_Gi", [128, 32])
        s_GrT = sb1("s_GrT", [32, 128]); s_GiT = sb1("s_GiT", [32, 128])
        s_K = sb1("s_K", [32, 128]); s_Kbf = sb1("s_Kbf", [32, 128], BF16)
        s_zrow = sb1("s_zrow", [1, 128], BF16)

        pP0 = mk1(nc.psum_tensor("pP0", [128, 512], F32))

        es_K = ExitStack()
        mkK = es_K.enter_context
        s_tch = mkK(nc.sbuf_tensor("s_tch", [64, 512], F32))
        s_di = mkK(nc.sbuf_tensor("s_di", [64, 512], F32))
        s_den = mkK(nc.sbuf_tensor("s_den", [64, 512], F32))
        s_rec = mkK(nc.sbuf_tensor("s_rec", [64, 512], F32))
        s_P = mkK(nc.sbuf_tensor("s_P", [64, 512], F32))
        s_Q = mkK(nc.sbuf_tensor("s_Q", [64, 512], F32))

        SCW = 1024
        es_big = ExitStack()
        mkbig = es_big.enter_context
        s_scf = [mkbig(nc.sbuf_tensor(f"s_scf{b}", [86, SCW], F32)) for b in range(2)]
        s_znib = [mkbig(nc.sbuf_tensor(f"s_znib{b}", [86, SCW], I16)) for b in range(2)]
        s_zw = [mkbig(nc.sbuf_tensor(f"s_zw{b}", [86, SCW // 8], I32)) for b in range(2)]
        s_scb = [mkbig(nc.sbuf_tensor(f"s_scb{b}", [86, SCW], BF16)) for b in range(2)]
        s_k8 = mkbig(nc.sbuf_tensor("s_k8", [8, L], F32))

        @block.sync
        def _(sync):
            for dst, src in [
                (s_lam[:], lam), (s_p[:], p_ri), (s_q[:], q_ri),
                (s_vc[:], vc.rearrange("m k c -> m (k c)")),
                (s_ct[:], ct_in.rearrange("o n c -> (o n) c")),
                (s_bp[:], bp), (s_lsr[:], lsr), (s_dd[:], dd_in),
                (s_cr[:], cr_pc), (s_ci[:], ci_pc),
                (s_CosM[:], CosM), (s_SinM[:], SinM), (s_SinMneg[:], SinMneg),
                (s_Wtr[:], Wtr), (s_Wti[:], Wti),
                (s_Er[:], Er), (s_Ei[:], Ei), (s_Eineg[:], Eineg),
                (s_id[:], ident),
            ]:
                LD.inc(sync.dma_start(out=dst, in_=src), 16)
        ld_consts = LD.n

        @block.vector
        def _(vector):
            vector.wait_ge(LD.h, ld_consts)
            CV.inc(vector.tensor_copy(out=s_idbf[:], in_=s_id[:]))
            marks["idbf"] = CV.n
            vector.wait_ge(CV.h, CV.n)

        stage_jobs = []

        def stage_one(tag, sc_dram, qz_dram, s_out, z_out, G, N, zs_mode):
            for c0 in range(0, N, SCW):
                cw_ = min(SCW, N - c0)
                stage_jobs.append((tag, sc_dram, qz_dram, s_out, z_out, G, N, zs_mode, c0, cw_))

        stage_ctr = [0]

        def emit_stage_chunk(job):
            (tag, sc_dram, qz_dram, s_out, z_out, G, N, zs_mode, c0, cw_) = job
            if True:
                ci = stage_ctr[0]
                stage_ctr[0] += 1
                pa = ci % 2
                scf, znib, zw, scb = s_scf[pa], s_znib[pa], s_zw[pa], s_scb[pa]

                @block.sync
                def _(sync):
                    if ci >= 2:
                        sync.wait_ge(DSS[pa].h, DSS[pa].n)
                    SLp[pa].inc(sync.dma_start(out=scf[0:G, 0:cw_], in_=sc_dram[:, c0:c0 + cw_]), 16)
                    SLp[pa].inc(sync.dma_start(out=zw[0:G, 0:cw_ // 8],
                                          in_=qz_dram[:, c0 // 8:(c0 + cw_) // 8]), 16)
                ldv = SLp[pa].n

                @block.vector
                def _(vector):
                    vector.wait_ge(SLp[pa].h, ldv)
                    CV.inc(vector.tensor_copy(out=scb[0:G, 0:cw_], in_=scf[0:G, 0:cw_]))
                    vector.wait_ge(CV.h, CV.n)
                cv_s = CV.n

                @block.sync
                def _(sync):
                    sync.wait_ge(CV.h, cv_s)
                    if G == 32:
                        DSS[pa].inc(sync.dma_start(
                            out=s_out[c0 // 128:(c0 + cw_) // 128, :, :].transpose([1, 0, 2]),
                            in_=scb[0:G, 0:cw_].rearrange("g (t n) -> g t n", n=128)), 16)
                    else:
                        DSS[pa].inc(sync.dma_start(
                            out=s_out[c0 // 512:(c0 + cw_) // 512, :, :].transpose([1, 0, 2]),
                            in_=scb[0:G, 0:cw_].rearrange("g (t n) -> g t n", n=512)), 16)
                ds_s = DSS[pa].n

                @block.vector
                def _(vector):
                    zv16 = zw[0:G, 0:cw_ // 8].bitcast(I16).rearrange("p (w h) -> p w h", h=2)
                    nv16 = znib[0:G, 0:cw_].rearrange("p (w e) -> p w e", e=8)
                    for sh in range(4):
                        CV.inc(vector.tensor_scalar(
                            out=nv16[:, :, 2 * sh:2 * sh + 2], in0=zv16,
                            scalar1=(4 * sh), scalar2=15,
                            op0=ALU.logical_shift_right, op1=ALU.bitwise_and))
                    vector.wait_ge(CV.h, CV.n)
                    vector.wait_ge(DSS[pa].h, ds_s)
                    if zs_mode:
                        CV.inc(vector.tensor_tensor(out=scb[0:G, 0:cw_], in0=znib[0:G, 0:cw_],
                                                    in1=scf[0:G, 0:cw_], op=ALU.mult))
                    else:
                        CV.inc(vector.tensor_copy(out=scb[0:G, 0:cw_], in_=znib[0:G, 0:cw_]))
                    vector.wait_ge(CV.h, CV.n)
                cv_z = CV.n

                @block.sync
                def _(sync):
                    sync.wait_ge(CV.h, cv_z)
                    if G == 32:
                        DSS[pa].inc(sync.dma_start(
                            out=z_out[c0 // 128:(c0 + cw_) // 128, :, :].transpose([1, 0, 2]),
                            in_=scb[0:G, 0:cw_].rearrange("g (t n) -> g t n", n=128)), 16)
                    else:
                        DSS[pa].inc(sync.dma_start(
                            out=z_out[c0 // 512:(c0 + cw_) // 512, :, :].transpose([1, 0, 2]),
                            in_=scb[0:G, 0:cw_].rearrange("g (t n) -> g t n", n=512)), 16)

        stage_one("g", gsc, gqz, ssc_g, zsc_g, 32, IM, False)
        stage_one("u", usc, uqz, ssc_u, zsc_u, 32, IM, False)
        stage_one("d", dsc, dqz, ssc_d, zsc_d, 86, DM, False)

        @block.scalar
        def _(scalar):
            scalar.wait_ge(LD.h, ld_consts)
            CS.inc(scalar.activation(out=s_step[:], in_=s_lsr[:], func=AF.Exp))
        cs_step = CS.n

        @block.vector
        def _(vector):
            vector.wait_ge(CS.h, cs_step)
            CV.inc(vector.reciprocal(out=s_s2[:], in_=s_step[:]))
            vector.wait_ge(CV.h, CV.n)
            CV.inc(vector.tensor_scalar(out=s_s2[:], in0=s_s2[:], scalar1=2.0, scalar2=None, op0=ALU.mult))
            CV.inc(vector.tensor_tensor(out=s_lr2[:], in0=s_lam[:, 0:1], in1=s_lam[:, 0:1], op=ALU.mult))
            CV.inc(vector.tensor_scalar(out=s_lrneg[:], in0=s_lam[:, 0:1], scalar1=-1.0, scalar2=None,
                                        op0=ALU.mult))
            vector.wait_ge(CV.h, CV.n)

        @block.tensor
        def _(tensor):
            tensor.wait_ge(LD.h, ld_consts)
            vcr = s_vc[:].rearrange("m (k c) -> m k c", c=2)
            tensor.transpose(out=pP0[0:64, 0:64], in_=vcr[:, :, 0], identity=s_id[0:64, 0:64])
            CT.inc(tensor.transpose(out=pP0[0:64, 64:128], in_=vcr[:, :, 1], identity=s_id[0:64, 0:64]))
        ct_vc = CT.n

        @block.scalar
        def _(scalar):
            scalar.wait_ge(CT.h, ct_vc)
            CS.inc(scalar.copy(out=s_vct[:], in_=pP0[0:64, 0:128]))
        cs_vct = CS.n

        @block.tensor
        def _(tensor):
            tensor.wait_ge(CS.h, cs_vct)
            tensor.matmul(out=pP0[0:64, 0:1], lhsT=s_vct[:, 0:64], rhs=s_bp[:], start=True, stop=True)
            CT.inc(tensor.matmul(out=pP0[0:64, 1:2], lhsT=s_vct[:, 64:128], rhs=s_bp[:], start=True, stop=True))
        ct_b = CT.n

        @block.scalar
        def _(scalar):
            scalar.wait_ge(CT.h, ct_b)
            CS.inc(scalar.copy(out=s_b[:], in_=pP0[0:64, 0:2]))
        cs_b = CS.n

        @block.vector
        def _(vector):
            vector.wait_ge(CS.h, cs_b)
            pairs = [(s_ct, s_b), (s_ct, s_p), (s_q, s_b), (s_q, s_p)]
            for j, (a, b2) in enumerate(pairs):
                ar, ai, br_, bi = a[:, 0:1], a[:, 1:2], b2[:, 0:1], b2[:, 1:2]
                CV.inc(vector.tensor_tensor(out=s_vt[:, 0:1], in0=ar, in1=br_, op=ALU.mult))
                CV.inc(vector.tensor_tensor(out=s_vt[:, 1:2], in0=ai, in1=bi, op=ALU.mult))
                CV.inc(vector.tensor_tensor(out=s_vt[:, 2:3], in0=ar, in1=bi, op=ALU.mult))
                CV.inc(vector.tensor_tensor(out=s_vt[:, 3:4], in0=ai, in1=br_, op=ALU.mult))
                vector.wait_ge(CV.h, CV.n)
                CV.inc(vector.tensor_tensor(out=s_vP[:, j:j + 1], in0=s_vt[:, 0:1], in1=s_vt[:, 1:2],
                                            op=ALU.add))
                CV.inc(vector.tensor_tensor(out=s_vP[:, 4 + j:5 + j], in0=s_vt[:, 2:3], in1=s_vt[:, 3:4],
                                            op=ALU.subtract))
                vector.wait_ge(CV.h, CV.n)
            CV.inc(vector.tensor_copy(out=s_vQ[:, 0:4], in_=s_vP[:, 4:8]))
            CV.inc(vector.tensor_scalar(out=s_vQ[:, 4:8], in0=s_vP[:, 0:4], scalar1=-1.0, scalar2=None,
                                        op0=ALU.mult))
        marks["cv_v"] = CV.n

        def emit_K_chunk(ch):
            @block.sync
            def _(sync):
                if ch == 0:
                    sync.wait_ge(CV.h, marks["cv_v"])
                else:
                    sync.wait_ge(CV.h, marks[f"pq{ch-1}"])
                LD.inc(sync.dma_start(out=s_tch[:],
                                      in_=t_row[0:1, 512 * ch:512 * (ch + 1)].to_broadcast((64, 512))), 16)
            ld_t = LD.n

            @block.vector
            def _(vector):
                vector.wait_ge(LD.h, ld_t)
                if ch:
                    vector.wait_ge(CS.h, marks[f"kev{ch-1}"])
                CV.inc(vector.tensor_scalar(out=s_di[:], in0=s_tch[:], scalar1=s_s2[:],
                                            scalar2=s_lam[:, 1:2], op0=ALU.mult, op1=ALU.subtract))
                vector.wait_ge(CV.h, CV.n)
                CV.inc(vector.tensor_tensor(out=s_den[:], in0=s_di[:], in1=s_di[:], op=ALU.mult))
                vector.wait_ge(CV.h, CV.n)
                CV.inc(vector.tensor_scalar(out=s_den[:], in0=s_den[:], scalar1=s_lr2[:], scalar2=None,
                                            op0=ALU.add))
                vector.wait_ge(CV.h, CV.n)
                CV.inc(vector.reciprocal(out=s_rec[:], in_=s_den[:]))
                vector.wait_ge(CV.h, CV.n)
                CV.inc(vector.tensor_scalar(out=s_P[:], in0=s_rec[:], scalar1=s_lrneg[:], scalar2=None,
                                            op0=ALU.mult))
                CV.inc(vector.tensor_tensor(out=s_Q[:], in0=s_di[:], in1=s_rec[:], op=ALU.mult))
                marks[f"pq{ch}"] = CV.n
                vector.wait_ge(CV.h, CV.n)

            @block.tensor
            def _(tensor):
                tensor.wait_ge(CV.h, marks[f"pq{ch}"])
                if ch:
                    tensor.wait_ge(CS.h, marks[f"kev{ch-1}"])
                tensor.matmul(out=pP0[0:8, 0:512], lhsT=s_vP[:], rhs=s_P[:], start=True, stop=False)
                CT.inc(tensor.matmul(out=pP0[0:8, 0:512], lhsT=s_vQ[:], rhs=s_Q[:], start=False, stop=True))
            ct_ch = CT.n

            @block.scalar
            def _(scalar):
                scalar.wait_ge(CT.h, ct_ch)
                CS.inc(scalar.copy(out=s_k8[:, 512 * ch:512 * (ch + 1)], in_=pP0[0:8, 0:512]))
                marks[f"kev{ch}"] = CS.n

        for j in range(max(len(stage_jobs), 8)):
            if j < len(stage_jobs):
                emit_stage_chunk(stage_jobs[j])
            if j < 8:
                emit_K_chunk(j)
        dss_done = (DSS[0].n, DSS[1].n)
        cs_k8 = CS.n

        @block.sync
        def _(sync):
            sync.wait_ge(CS.h, cs_k8)
            DS.inc(sync.dma_start(out=kscr, in_=s_k8[:]), 16)
            sync.wait_ge(DS.h, DS.n)
            for i in range(8):
                DS.inc(sync.dma_start(out=kk[i][:],
                                      in_=kscr[i:i + 1, :].rearrange("o (p c) -> (o p) c", p=128)), 16)
        ds_kk = DS.n
        es_big.close()
        es_K.close()

        @block.vector
        def _(vector):
            vector.wait_ge(DS.h, ds_kk)
            kr00, kr01, kr10, kr11 = kk[0][:], kk[1][:], kk[2][:], kk[3][:]
            ki00, ki01, ki10, ki11 = kk[4][:], kk[5][:], kk[6][:], kk[7][:]
            d1, num_r, num_i, den2, rden, zr, zi, wr_, wi_, t9 = [c[:] for c in cw]

            def tt(o, a, b2, op):
                CV.inc(vector.tensor_tensor(out=o, in0=a, in1=b2, op=op))

            def w():
                vector.wait_ge(CV.h, CV.n)

            CV.inc(vector.tensor_scalar(out=d1, in0=kr11, scalar1=1.0, scalar2=None, op0=ALU.add))
            tt(num_r, kr01, kr10, ALU.mult); tt(t9, ki01, ki10, ALU.mult); w()
            tt(num_r, num_r, t9, ALU.subtract)
            tt(num_i, kr01, ki10, ALU.mult); w(); tt(t9, ki01, kr10, ALU.mult); w()
            tt(num_i, num_i, t9, ALU.add)
            tt(den2, d1, d1, ALU.mult); w(); tt(t9, ki11, ki11, ALU.mult); w()
            tt(den2, den2, t9, ALU.add); w()
            CV.inc(vector.reciprocal(out=rden, in_=den2)); w()
            tt(zr, num_r, d1, ALU.mult); tt(t9, num_i, ki11, ALU.mult); w()
            tt(zr, zr, t9, ALU.add)
            tt(zi, num_i, d1, ALU.mult); w(); tt(t9, num_r, ki11, ALU.mult); w()
            tt(zi, zi, t9, ALU.subtract); w()
            tt(zr, zr, rden, ALU.mult); tt(zi, zi, rden, ALU.mult); w()
            tt(wr_, kr00, zr, ALU.subtract); tt(wi_, ki00, zi, ALU.subtract); w()
            tt(s_atr[:], s_cr[:], wr_, ALU.mult); tt(t9, s_ci[:], wi_, ALU.mult); w()
            tt(s_atr[:], s_atr[:], t9, ALU.subtract)
            tt(s_ati[:], s_cr[:], wi_, ALU.mult); w(); tt(t9, s_ci[:], wr_, ALU.mult); w()
            tt(s_ati[:], s_ati[:], t9, ALU.add); w()
        cv_at = CV.n

        @block.tensor
        def _(tensor):
            tensor.wait_ge(CV.h, cv_at)
            tensor.matmul(out=pP0[:, 0:32], lhsT=s_CosM[:], rhs=s_atr[:], start=True, stop=False)
            tensor.matmul(out=pP0[:, 0:32], lhsT=s_SinMneg[:], rhs=s_ati[:], start=False, stop=True)
            tensor.matmul(out=pP0[:, 32:64], lhsT=s_CosM[:], rhs=s_ati[:], start=True, stop=False)
            CT.inc(tensor.matmul(out=pP0[:, 32:64], lhsT=s_SinM[:], rhs=s_atr[:], start=False, stop=True))
        ct_f = CT.n

        @block.scalar
        def _(scalar):
            scalar.wait_ge(CT.h, ct_f)
            CS.inc(scalar.copy(out=s_Fr[:], in_=pP0[:, 0:32]))
            CS.inc(scalar.copy(out=s_Fi[:], in_=pP0[:, 32:64]))
        cs_f = CS.n

        @block.vector
        def _(vector):
            def tt(o, a, b2, op):
                CV.inc(vector.tensor_tensor(out=o, in0=a, in1=b2, op=op))

            vector.wait_ge(CS.h, cs_f)
            tt(s_Gr[:], s_Fr[:], s_Wtr[:], ALU.mult); tt(cw[9][:], s_Fi[:], s_Wti[:], ALU.mult)
            vector.wait_ge(CV.h, CV.n)
            tt(s_Gr[:], s_Gr[:], cw[9][:], ALU.subtract)
            tt(s_Gi[:], s_Fr[:], s_Wti[:], ALU.mult)
            vector.wait_ge(CV.h, CV.n)
            tt(cw[9][:], s_Fi[:], s_Wtr[:], ALU.mult)
            vector.wait_ge(CV.h, CV.n)
            tt(s_Gi[:], s_Gi[:], cw[9][:], ALU.add)
            vector.wait_ge(CV.h, CV.n)
        cv_g = CV.n

        @block.tensor
        def _(tensor):
            tensor.wait_ge(CV.h, cv_g)
            tensor.transpose(out=pP0[0:32, 0:128], in_=s_Gr[:], identity=s_id[:])
            CT.inc(tensor.transpose(out=pP0[0:32, 128:256], in_=s_Gi[:], identity=s_id[:]))
        ct_gt = CT.n

        @block.scalar
        def _(scalar):
            scalar.wait_ge(CT.h, ct_gt)
            CS.inc(scalar.copy(out=s_GrT[:], in_=pP0[0:32, 0:128]))
            CS.inc(scalar.copy(out=s_GiT[:], in_=pP0[0:32, 128:256]))
        cs_gt = CS.n

        @block.tensor
        def _(tensor):
            tensor.wait_ge(CS.h, cs_gt)
            tensor.matmul(out=pP0[0:32, 0:128], lhsT=s_Er[:], rhs=s_GrT[:], start=True, stop=False)
            CT.inc(tensor.matmul(out=pP0[0:32, 0:128], lhsT=s_Eineg[:], rhs=s_GiT[:], start=False, stop=True))
        ct_k = CT.n

        @block.scalar
        def _(scalar):
            scalar.wait_ge(CT.h, ct_k)
            CS.inc(scalar.copy(out=s_K[:], in_=pP0[0:32, 0:128]))
        cs_kk2 = CS.n

        @block.vector
        def _(vector):
            vector.wait_ge(CS.h, cs_kk2)
            CV.inc(vector.tensor_copy(out=s_Kbf[:], in_=s_K[:]))
            CV.inc(vector.memset(s_zrow[:], 0.0))
            vector.wait_ge(CV.h, CV.n)
        cv_kbf = CV.n

        @block.sync
        def _(sync):
            sync.wait_ge(CV.h, cv_kbf)
            DS.inc(sync.dma_start(out=kpad[0:1, 0:128], in_=s_zrow[:]), 16)
            DS.inc(sync.dma_start(out=kpad[0:1, 128:128 + L].rearrange("o (a b) -> (o a) b", a=32),
                                  in_=s_Kbf[:]), 16)
        ds_kpad = DS.n

        @block.gpsimd
        def _(gpsimd):
            gpsimd.wait_ge(DS.h, ds_kpad)
            for r in range(128):
                TB.inc(gpsimd.dma_start(out=s_T[r:r + 1, :], in_=kpad[0:1, 128 - r:128 - r + L]), 16)
        ds_T = TB.n

        es_p1 = ExitStack()
        mkp1 = es_p1.enter_context
        s_x = [mkp1(nc.sbuf_tensor(f"s_x{b}", [128, DM], F32)) for b in range(2)]
        s_xnb = [mkp1(nc.sbuf_tensor(f"s_xnb{b}", [128, DM], BF16)) for b in range(2)]
        s_st6 = [mkp1(nc.sbuf_tensor(f"s_st6{b}", [128, 48], F32)) for b in range(2)]
        s_mv = [mkp1(nc.sbuf_tensor(f"s_mv{b}", [128, 2], F32)) for b in range(2)]
        s_v1 = [mkp1(nc.sbuf_tensor(f"s_v1{b}", [128, 1], F32)) for b in range(2)]
        s_v2 = [mkp1(nc.sbuf_tensor(f"s_v2{b}", [128, 1], F32)) for b in range(2)]
        s_cv = mkp1(nc.sbuf_tensor("s_cv", [128, 2048], F32))

        ptr_ps = mkp1(nc.psum_tensor("ptr_ps", [128, 2048], BF16))
        pcv = mkp1(nc.psum_tensor("pcv", [128, 2048], F32))

        xl_mark = [0] * NMT
        xn_mark = [0] * NMT
        emit_conv_holder = [None]

        def emit_conv(mt):
            for half in range(2):
                @block.tensor
                def _(tensor):
                    tensor.wait_ge(CS.h, marks[(mt, 1)])
                    if mt == 0 and half == 0:
                        tensor.wait_ge(TB.h, ds_T)
                    ckey = ("cv", mt, half - 1) if half else ("cv", mt - 1, 1)
                    if ckey in marks:
                        tensor.wait_ge(CS.h, marks[ckey])
                    ilo_h = 16 * half
                    args = []
                    for j_blk in range(ilo_h + 16):
                        for bl in range(4):
                            i_lo = max(ilo_h + 4 * bl, j_blk)
                            i_hi = ilo_h + 4 * bl + 4
                            if i_lo < i_hi:
                                args.append((j_blk, i_lo, i_hi))
                    for idx, (j_blk, i_lo, i_hi) in enumerate(args):
                        d_lo = i_lo - j_blk
                        mm = tensor.matmul(
                            out=pcv[:, (i_lo - ilo_h) * 128:(i_hi - ilo_h) * 128],
                            lhsT=s_xnT[:, MROWS * j_blk + 128 * mt: MROWS * j_blk + 128 * (mt + 1)],
                            rhs=s_T[:, 128 * d_lo:128 * (d_lo + (i_hi - i_lo))],
                            start=(j_blk == 0), stop=(j_blk == i_hi - 1))
                        if idx == len(args) - 1:
                            CT.inc(mm)
                marks[("cvt", mt, half)] = CT.n

                @block.scalar
                def _(scalar):
                    scalar.wait_ge(CT.h, marks[("cvt", mt, half)])
                    if mt or half:
                        scalar.wait_ge(CVS.h, CVS.n)
                    CS.inc(scalar.copy(out=s_cv[:], in_=pcv[:]))
                marks[("cv", mt, half)] = CS.n

                @block.sync
                def _(sync):
                    sync.wait_ge(CS.h, marks[("cv", mt, half)])
                    CVS.inc(sync.dma_start(
                        out=cscr[128 * mt:128 * (mt + 1), 2048 * half:2048 * (half + 1)],
                        in_=s_cv[:]), 16)


        emit_conv_holder[0] = emit_conv


        for mt in range(NMT):
            @block.sync
            def _(sync):
                if mt == 0:
                    sync.wait_ge(CS.h, cs_k8)
                    sync.wait_ge(DS.h, ds_kk)
                    sync.wait_ge(DSS[0].h, dss_done[0])
                    sync.wait_ge(DSS[1].h, dss_done[1])
                if mt >= 2:
                    sync.wait_ge(CS.h, xn_mark[mt - 2])
                XQp[mt % 2].inc(sync.dma_start(out=s_x[mt % 2][:], in_=x_in[128 * mt:128 * (mt + 1), :]), 16)
                xl_mark[mt] = XQp[mt % 2].n

            @block.vector
            def _(vector):
                vector.wait_ge(XQp[mt % 2].h, xl_mark[mt])
                if mt >= 2:
                    vector.wait_ge(CS.h, marks[f"rstd{mt-2}"])
                for c8 in range(8):
                    CV.inc(vector.bn_stats(out=s_st6[mt % 2][:, 6 * c8:6 * (c8 + 1)],
                                           in_=s_x[mt % 2][:, 512 * c8:512 * (c8 + 1)]))
                vector.wait_ge(CV.h, CV.n)
                CV.inc(vector.bn_aggr(out=s_mv[mt % 2][:], in_=s_st6[mt % 2][:]))
                vector.wait_ge(CV.h, CV.n)
                CV.inc(vector.tensor_scalar(out=s_v1[mt % 2][:], in0=s_mv[mt % 2][:, 1:2], scalar1=1e-5,
                                            scalar2=None, op0=ALU.add))
                vector.wait_ge(CV.h, CV.n)
                CV.inc(vector.reciprocal(out=s_v2[mt % 2][:], in_=s_v1[mt % 2][:]))
                vector.wait_ge(CV.h, CV.n)
            marks[f"ln{mt}"] = CV.n

            @block.scalar
            def _(scalar):
                scalar.wait_ge(CV.h, marks[f"ln{mt}"])
                CS.inc(scalar.activation(out=s_rstd[:, mt:mt + 1], in_=s_v2[mt % 2][:], func=AF.Sqrt))
            marks[f"sq{mt}"] = CS.n

            @block.vector
            def _(vector):
                vector.wait_ge(CS.h, marks[f"sq{mt}"])
                CV.inc(vector.tensor_tensor(out=s_v1[mt % 2][:], in0=s_mv[mt % 2][:, 0:1],
                                            in1=s_rstd[:, mt:mt + 1], op=ALU.mult))
                vector.wait_ge(CV.h, CV.n)
                CV.inc(vector.tensor_scalar(out=s_nbias[:, mt:mt + 1], in0=s_v1[mt % 2][:], scalar1=-1.0,
                                            scalar2=None, op0=ALU.mult))
                vector.wait_ge(CV.h, CV.n)
            marks[f"nb{mt}"] = CV.n

            @block.scalar
            def _(scalar):
                scalar.wait_ge(CV.h, marks[f"nb{mt}"])
                if (mt - 2, 1, "t") in marks:
                    scalar.wait_ge(CT.h, marks[(mt - 2, 1, "t")])
                CS.inc(scalar.activation(out=s_xnb[mt % 2][:], in_=s_x[mt % 2][:], func=AF.Identity,
                                         bias=s_nbias[:, mt:mt + 1], scale=s_rstd[:, mt:mt + 1]))
                marks[f"rstd{mt}"] = CS.n
                xn_mark[mt] = CS.n

            for rnd in range(2):
                @block.tensor
                def _(tensor):
                    tensor.wait_ge(CS.h, xn_mark[mt])
                    if mt == 0 and rnd == 0:
                        tensor.wait_ge(CV.h, marks["idbf"])
                    key = (mt, rnd - 1) if rnd else (mt - 1, 1)
                    if key in marks:
                        tensor.wait_ge(CS.h, marks[key])
                    for ft16 in range(16):
                        ft = 16 * rnd + ft16
                        mm = tensor.transpose(out=ptr_ps[:, 128 * ft16:128 * (ft16 + 1)],
                                              in_=s_xnb[mt % 2][:, 128 * ft:128 * (ft + 1)],
                                              identity=s_idbf[:])
                        if ft16 == 15:
                            CT.inc(mm)
                marks[(mt, rnd, "t")] = CT.n

                @block.scalar
                def _(scalar):
                    scalar.wait_ge(CT.h, marks[(mt, rnd, "t")])
                    xv = s_xnT[:].rearrange("p (K m) -> p K m", m=MROWS)
                    CS.inc(scalar.copy(
                        out=xv[:, 16 * rnd:16 * (rnd + 1), 128 * mt:128 * (mt + 1)],
                        in_=ptr_ps[:].rearrange("p (K c) -> p K c", c=128)))
                marks[(mt, rnd)] = CS.n

            if mt >= 8:
                emit_conv_holder[0](mt - 8)

        for mt_c in range(NMT - 8, NMT):
            emit_conv_holder[0](mt_c)

        cs_p1 = CS.n
        cvs_conv = CVS.n

        es_p1.close()
        es1.close()
        es_T.close()

        esA = ExitStack()
        mkA = esA.enter_context
        sbA = lambda n, s, d=F32: mkA(nc.sbuf_tensor(n, s, d))

        ig = [sbA(f"ig{b}", [128, KT * 16], I32) for b in range(2)]
        iu = [sbA(f"iu{b}", [128, KT * 16], I32) for b in range(2)]
        wg = [sbA(f"wg{b}", [128, KT * 128], BF16) for b in range(2)]
        wu = [sbA(f"wu{b}", [128, KT * 128], BF16) for b in range(2)]
        reps = {(ty, ma, qq): sbA(f"rep_{ty}{ma}{qq}", [128, 1024], BF16)
                for ty in "sz" for ma in "gu" for qq in range(3)}
        s_nib = sbA("s_nib", [128, 1024], I16)
        s_tmpb = sbA("s_tmpb", [128, 1024], BF16)
        hb = [sbA(f"hb{b}", [128, MROWS], BF16) for b in range(2)]

        pg = mkA(nc.psum_tensor("pg", [128, MROWS], F32))
        pu = mkA(nc.psum_tensor("pu", [128, MROWS], F32))

        iq_mark = {}
        bc_mark = {}
        vd_mark = {}
        vd_mark_q = {}
        exd_mark = {}
        tg_mark = [0] * (NNT + 1)
        tu_mark = [0] * (NNT + 1)
        silu_mark = [0] * NNT
        hmul_mark = [0] * NNT
        hspill_mark = [0] * NNT
        first_bc = [True]

        def emit_A_loads(t):
            @block.sync
            def _(sync):
                if t == 0:
                    sync.wait_ge(CS.h, cs_p1)
                if t >= 2:
                    sync.wait_ge(CV.h, exd_mark[t - 2])
                for mat, src, buf in [("g", gqw, ig[t % 2]), ("u", uqw, iu[t % 2])]:
                    v = src.rearrange("(K p) w -> p K w", p=128)[:, :, 16 * t:16 * (t + 1)]
                    dv = buf[:].rearrange("p (K w) -> p K w", w=16)
                    IQp[t % 2].inc(sync.dma_start(out=dv[:, 0:16, :], in_=v[:, 0:16, :]), 16)
                    IQp[t % 2].inc(sync.dma_start(out=dv[:, 16:32, :], in_=v[:, 16:32, :]), 16)
                iq_mark[t] = IQp[t % 2].n

        def emit_one_bc(scalar, t, mat, q):
            sscr, zscr = {"g": (ssc_g, zsc_g), "u": (ssc_u, zsc_u)}[mat]
            BCm = {"g": BCg, "u": BCu}[mat][q % 3]
            if q >= 3:
                prev = (t, mat, 0)
            elif q == 0:
                prev = (t - 1, mat, 3)
            else:
                prev = (t - 1, mat, q)
            if prev in vd_mark_q:
                scalar.wait_ge(CV.h, vd_mark_q[prev])
            BCm.inc(scalar.dma_start(
                out=reps[("s", mat, q % 3)][:].rearrange("p (g n) -> p g n", g=8),
                in_=sscr[t, 8 * q:8 * (q + 1), :].partition_broadcast(128)), 16)
            BCm.inc(scalar.dma_start(
                out=reps[("z", mat, q % 3)][:].rearrange("p (g n) -> p g n", g=8),
                in_=zscr[t, 8 * q:8 * (q + 1), :].partition_broadcast(128)), 16)
            bc_mark[(t, mat, q)] = BCm.n

        def emit_A_bc(t):
            @block.scalar
            def _(scalar):
                if first_bc[0]:
                    scalar.wait_ge(DS.h, ds_kpad)
                    scalar.wait_ge(DSS[0].h, dss_done[0])
                    scalar.wait_ge(DSS[1].h, dss_done[1])
                    scalar.wait_ge(CS.h, cs_p1)
                    scalar.wait_ge(CVS.h, cvs_conv)
                    first_bc[0] = False
                for q in range(3):
                    for mat in "gu":
                        emit_one_bc(scalar, t, mat, q)

        def emit_A_deq_q(t, mat, qs):
            @block.vector
            def _(vector):
                ibuf = {"g": ig, "u": iu}[mat][t % 2]
                wbuf = {"g": wg, "u": wu}[mat][t % 2]
                if qs[0] == 0:
                    vector.wait_ge(IQp[t % 2].h, iq_mark[t])
                    if t >= 2:
                        vector.wait_ge(CT.h, tu_mark[t - 2] if mat == "u" else tg_mark[t - 2])
                iv = ibuf[:].bitcast(I16).rearrange("p (K w h) -> p K w h", w=16, h=2)
                nv = s_nib[:].rearrange("p (K w e) -> p K w e", w=16, e=8)
                for q in qs:
                    vector.wait_ge({"g": BCg, "u": BCu}[mat][q % 3].h, bc_mark[(t, mat, q)])
                    for sh in range(4):
                        CV.inc(vector.tensor_scalar(
                            out=nv[:, 0:8, :, 2 * sh:2 * sh + 2],
                            in0=iv[:, 8 * q:8 * (q + 1), :, :],
                            scalar1=(4 * sh), scalar2=15,
                            op0=ALU.logical_shift_right, op1=ALU.bitwise_and))
                    vector.wait_ge(CV.h, CV.n)
                    CV.inc(vector.tensor_tensor(out=s_tmpb[:], in0=s_nib[:],
                                                in1=reps[("z", mat, q % 3)][:], op=ALU.subtract))
                    vector.wait_ge(CV.h, CV.n)
                    CV.inc(vector.tensor_tensor(out=wbuf[:, 1024 * q:1024 * (q + 1)], in0=s_tmpb[:],
                                                in1=reps[("s", mat, q % 3)][:], op=ALU.mult))
                    vd_mark_q[(t, mat, q)] = CV.n
                    vector.wait_ge(CV.h, CV.n)
                if qs[-1] == 3:
                    vd_mark[(t, mat)] = CV.n
                    exd_mark[t] = CV.n

        def emit_A_deq(t):
            for mat in "gu":
                emit_A_deq_q(t, mat, [0])

                @block.scalar
                def _(scalar):
                    emit_one_bc(scalar, t, mat, 3)

                emit_A_deq_q(t, mat, [1, 2, 3])

        def emit_A_mms(t):
            @block.tensor
            def _(tensor):
                tensor.wait_ge(CV.h, vd_mark[(t, "g")])
                if t == 0:
                    tensor.wait_ge(CS.h, cs_p1)
                if t >= 1:
                    tensor.wait_ge(CS.h, silu_mark[t - 1])
                for K in range(KT):
                    for c in range(4):
                        mm = tensor.matmul(out=pg[:, 512 * c:512 * (c + 1)],
                                           lhsT=wg[t % 2][:, 128 * K:128 * (K + 1)],
                                           rhs=s_xnT[:, MROWS * K + 512 * c: MROWS * K + 512 * (c + 1)],
                                           start=(K == 0), stop=(K == KT - 1))
                        if K == KT - 1 and c == 3:
                            CT.inc(mm)
                tg_mark[t] = CT.n
                tensor.wait_ge(CV.h, vd_mark[(t, "u")])
                if t >= 1:
                    tensor.wait_ge(CV.h, hmul_mark[t - 1])
                for K in range(KT):
                    for c in range(4):
                        mm = tensor.matmul(out=pu[:, 512 * c:512 * (c + 1)],
                                           lhsT=wu[t % 2][:, 128 * K:128 * (K + 1)],
                                           rhs=s_xnT[:, MROWS * K + 512 * c: MROWS * K + 512 * (c + 1)],
                                           start=(K == 0), stop=(K == KT - 1))
                        if K == KT - 1 and c == 3:
                            CT.inc(mm)
                tu_mark[t] = CT.n

        def emit_A_silu(t):
            @block.scalar
            def _(scalar):
                scalar.wait_ge(CT.h, tg_mark[t])
                if t >= 2:
                    scalar.wait_ge(HSp[t % 2].h, hspill_mark[t - 2])
                CS.inc(scalar.activation(out=hb[t % 2][:], in_=pg[:], func=AF.Silu))
                silu_mark[t] = CS.n

        def emit_A_hmul(t):
            @block.vector
            def _(vector):
                vector.wait_ge(CS.h, silu_mark[t])
                vector.wait_ge(CT.h, tu_mark[t])
                CV.inc(vector.tensor_tensor(out=hb[t % 2][:], in0=hb[t % 2][:], in1=pu[:], op=ALU.mult))
                hmul_mark[t] = CV.n
                vector.wait_ge(CV.h, CV.n)

        def emit_A_spill(t):
            @block.sync
            def _(sync):
                sync.wait_ge(CV.h, hmul_mark[t])
                HSp[t % 2].inc(sync.dma_start(
                    out=hscr[:, :, t, :].transpose([1, 0, 2]),
                    in_=hb[t % 2][:].rearrange("p (q m) -> p q m", q=4)), 16)
                hspill_mark[t] = HSp[t % 2].n

        emit_A_loads(0)
        emit_A_bc(0)
        for t in range(NNT):
            if t + 1 < NNT:
                emit_A_loads(t + 1)
            emit_A_deq(t)
            if t >= 1:
                emit_A_hmul(t - 1)
            if t + 1 < NNT:
                emit_A_bc(t + 1)
            emit_A_mms(t)
            emit_A_silu(t)
            if t >= 1:
                emit_A_spill(t - 1)
        emit_A_hmul(NNT - 1)
        emit_A_spill(NNT - 1)
        cv_A = CV.n
        hs_A = (HSp[0].n, HSp[1].n)

        esA.close()
        es_xnT.close()

        esB = ExitStack()
        mkB = esB.enter_context
        sbB = lambda n, s, d=F32: mkB(nc.sbuf_tensor(n, s, d))

        wd_slab = sbB("wd_slab", [128, NNT * 512], BF16)
        id4b = [sbB(f"id4b{b}", [128, BW * 64], I32) for b in range(2)]
        rps = [sbB(f"rps{b}", [128, BW * 512], BF16) for b in range(2)]
        rpz = [sbB(f"rpz{b}", [128, BW * 512], BF16) for b in range(2)]
        s_nib2 = sbB("s_nib2", [128, BW * 512], I16)
        HBW = 8
        NHBUF = 4
        hbl = [sbB(f"hbl{b}", [128, HBW * 512], BF16) for b in range(NHBUF)]
        xcb = [sbB(f"xcb{j}", [128, 512]) for j in range(8)]
        ccb = [sbB(f"ccb{j}", [128, 512]) for j in range(8)]
        xnc = [sbB(f"xnc{j}", [128, 512]) for j in range(2)]
        ob1 = [sbB(f"ob1{j}", [128, 512]) for j in range(2)]
        ob2 = [sbB(f"ob2{j}", [128, 512]) for j in range(2)]

        pd = [mkB(nc.psum_tensor(f"pd{i}", [128, 512], F32)) for i in range(8)]

        iq2 = {}
        bc2 = {}
        vd2b = {}
        hl2 = {}
        td2 = {}
        comb_v = {}
        xc_mark = {}

        def bof(i):
            return i // BW

        def emit_B_deq_batch(e, b):
            i0 = BSTARTS[b]
            bw = min(BW, NNT - i0)

            @block.scalar
            def _(scalar):
                if e == 0 and b == 0:
                    scalar.wait_ge(CV.h, cv_A)
                    scalar.wait_ge(HSp[0].h, hs_A[0])
                    scalar.wait_ge(HSp[1].h, hs_A[1])
                prevp = NB - 1 if (NB - 1) % 2 == b % 2 else NB - 2
                kb = (e, b - 2) if b >= 2 else (e - 1, prevp)
                if kb in vd2b:
                    scalar.wait_ge(CV.h, vd2b[kb])
                IQp[b % 2].inc(scalar.dma_start(
                    out=id4b[b % 2][:, 0:bw * 64].rearrange("p (I w) -> p I w", w=64),
                    in_=dqw[128 * i0:128 * (i0 + bw), 64 * e:64 * (e + 1)].rearrange(
                        "(I p) w -> p I w", p=128)), 16)
                iq2[(e, b)] = IQp[b % 2].n
                BCd[b % 2].inc(scalar.dma_start(
                    out=rps[b % 2][:, 0:bw * 512].rearrange("p (I n) -> p I n", n=512),
                    in_=ssc_d[e, i0:i0 + bw, :].partition_broadcast(128)), 16)
                BCd[b % 2].inc(scalar.dma_start(
                    out=rpz[b % 2][:, 0:bw * 512].rearrange("p (I n) -> p I n", n=512),
                    in_=zsc_d[e, i0:i0 + bw, :].partition_broadcast(128)), 16)
                bc2[(e, b)] = BCd[b % 2].n

            @block.vector
            def _(vector):
                vector.wait_ge(IQp[b % 2].h, iq2[(e, b)])
                vector.wait_ge(BCd[b % 2].h, bc2[(e, b)])
                if e >= 1:
                    vector.wait_ge(CT.h, td2[(e - 1, 3, i0 + bw - 1)])
                iv = id4b[b % 2][:].bitcast(I16).rearrange("p (I w h) -> p I w h", w=64, h=2)
                nv = s_nib2[:].rearrange("p (I w e) -> p I w e", w=64, e=8)
                for sh in range(4):
                    CV.inc(vector.tensor_scalar(
                        out=nv[:, 0:bw, :, 2 * sh:2 * sh + 2],
                        in0=iv[:, 0:bw, :, :],
                        scalar1=(4 * sh), scalar2=15,
                        op0=ALU.logical_shift_right, op1=ALU.bitwise_and))
                vector.wait_ge(CV.h, CV.n)
                tb2 = s_nib2[:].bitcast(BF16)
                CV.inc(vector.tensor_tensor(out=tb2[:, 0:bw * 512], in0=s_nib2[:, 0:bw * 512],
                                            in1=rpz[b % 2][:, 0:bw * 512], op=ALU.subtract))
                vector.wait_ge(CV.h, CV.n)
                CV.inc(vector.tensor_tensor(out=wd_slab[:, 512 * i0:512 * (i0 + bw)],
                                            in0=tb2[:, 0:bw * 512],
                                            in1=rps[b % 2][:, 0:bw * 512], op=ALU.mult))
                vd2b[(e, b)] = CV.n
                vector.wait_ge(CV.h, CV.n)

        NHB = (NNT + HBW - 1) // HBW

        def emit_B_quad(e, quad):
            @block.gpsimd
            def _(gpsimd):
                pkey = (e, quad - 2) if quad >= 2 else (e - 1, quad + 2)
                if pkey in comb_v:
                    gpsimd.wait_ge(CV.h, comb_v[pkey])
                if e == 0 and quad == 0:
                    gpsimd.wait_ge(CVS.h, cvs_conv)
                    gpsimd.wait_ge(CV.h, cv_A)
                qp = 4 * (quad % 2)
                for jj in range(4):
                    mt = 4 * quad + jj
                    XG.inc(gpsimd.dma_start(
                        out=xcb[qp + jj][:],
                        in_=x_in[128 * mt:128 * (mt + 1), 512 * e:512 * (e + 1)]), 16)
                    XG.inc(gpsimd.dma_start(
                        out=ccb[qp + jj][:],
                        in_=cscr[128 * mt:128 * (mt + 1), 512 * e:512 * (e + 1)]), 16)
                xc_mark[(e, quad)] = XG.n

            pbase = 4 * (quad % 2)
            for hB in range(NHB):
                i0h = HBW * hB
                hw = min(HBW, NNT - i0h)

                @block.sync
                def _(sync):
                    if hB >= NHBUF:
                        pb2 = HBW * (hB - NHBUF)
                        sync.wait_ge(CT.h, td2[(e, quad, pb2 + min(HBW, NNT - pb2) - 1)])
                    else:
                        qk = (e, quad - 1) if quad >= 1 else (e - 1, 3)
                        prev_b = NHB - 1 - ((NHB - 1 - hB) % NHBUF)
                        if prev_b > NHB - 1 - NHBUF:
                            pass
                        prev_b = max(b for b in range(NB + NHB) if b < NHB and b % NHBUF == hB % NHBUF)
                        if (qk[0], qk[1], NNT - 1) in td2:
                            last_i = min(HBW * prev_b + HBW, NNT) - 1
                            sync.wait_ge(CT.h, td2[(qk[0], qk[1], last_i)])
                    HLp[hB % NHBUF].inc(sync.dma_start(
                        out=hbl[hB % NHBUF][:, 0:hw * 512].rearrange("p (I m) -> p I m", m=512),
                        in_=hscr[quad, :, i0h:i0h + hw, :]), 16)
                    hl2[(e, quad, hB)] = HLp[hB % NHBUF].n

                @block.tensor
                def _(tensor):
                    tensor.wait_ge(HLp[hB % NHBUF].h, hl2[(e, quad, hB)])
                    for i in range(i0h, i0h + hw):
                        if i == 0 or bof(i) != bof(i - 1):
                            tensor.wait_ge(CV.h, vd2b[(e, bof(i))])
                        if i == 0:
                            if e == 0 and quad == 0:
                                tensor.wait_ge(CV.h, cv_A)
                            key = (e, quad - 2) if quad >= 2 else (e - 1, quad + 2)
                            if key in comb_v:
                                tensor.wait_ge(CV.h, comb_v[key])
                        for jj in range(4):
                            mm = tensor.matmul(out=pd[pbase + jj][:],
                                               lhsT=hbl[hB % NHBUF][:, 512 * (i - i0h) + 128 * jj:
                                                                    512 * (i - i0h) + 128 * (jj + 1)],
                                               rhs=wd_slab[:, 512 * i:512 * (i + 1)],
                                               start=(i == 0), stop=(i == NNT - 1))
                            if jj == 3:
                                CT.inc(mm)
                        td2[(e, quad, i)] = CT.n

        def emit_B_combine(e, quad):
            qp = 4 * (quad % 2)
            for jj in range(4):
                mt = 4 * quad + jj

                @block.vector
                def _(vector):
                    vector.wait_ge(CT.h, td2[(e, quad, NNT - 1)])
                    vector.wait_ge(XG.h, xc_mark[(e, quad)])
                    ok = (e, quad, jj - 2, "so") if jj >= 2 else (
                        (e, quad - 1, jj + 2, "so") if quad >= 1 else (e - 1, 3, jj + 2, "so"))
                    if ok in marks:
                        vector.wait_ge(SOp[jj % 2].h, marks[ok])
                    CV.inc(vector.tensor_scalar(out=xnc[jj % 2][:], in0=xcb[qp + jj][:],
                                                scalar1=s_rstd[:, mt:mt + 1],
                                                scalar2=s_nbias[:, mt:mt + 1],
                                                op0=ALU.mult, op1=ALU.add))
                    CV.inc(vector.tensor_tensor(out=ob1[jj % 2][:], in0=pd[qp + jj][:],
                                                in1=xcb[qp + jj][:], op=ALU.add))
                    vector.wait_ge(CV.h, CV.n)
                    CV.inc(vector.scalar_tensor_tensor(out=ob2[jj % 2][:], in0=xnc[jj % 2][:],
                                                       scalar=s_dd[:], in1=ob1[jj % 2][:],
                                                       op0=ALU.mult, op1=ALU.add))
                    vector.wait_ge(CV.h, CV.n)
                    CV.inc(vector.tensor_tensor(out=ob1[jj % 2][:], in0=ob2[jj % 2][:],
                                                in1=ccb[qp + jj][:], op=ALU.add))
                    marks[(e, quad, jj)] = CV.n
                    vector.wait_ge(CV.h, CV.n)
                    if jj == 3:
                        comb_v[(e, quad)] = CV.n

                @block.gpsimd
                def _(gpsimd):
                    gpsimd.wait_ge(CV.h, marks[(e, quad, jj)])
                    SOp[jj % 2].inc(gpsimd.dma_start(
                        out=out[128 * mt:128 * (mt + 1), 512 * e:512 * (e + 1)],
                        in_=ob1[jj % 2][:]), 16)
                    marks[(e, quad, jj, "so")] = SOp[jj % 2].n

        for b in range(NB):
            emit_B_deq_batch(0, b)
        for e in range(8):
            for quad in range(4):
                emit_B_quad(e, quad)
                if quad == 3 and e + 1 < 8:
                    for b in range(0, 9):
                        emit_B_deq_batch(e + 1, b)
                emit_B_combine(e, quad)
            if e + 1 < 8:
                for b in range(9, NB):
                    emit_B_deq_batch(e + 1, b)

        @block.gpsimd
        def _(gpsimd):
            gpsimd.wait_ge(SOp[0].h, SOp[0].n)
            gpsimd.wait_ge(SOp[1].h, SOp[1].n)

        esB.close()

    return nc


_BUILD_CACHE = {}


def _get_nc():
    if "nc" not in _BUILD_CACHE:
        _BUILD_CACHE["nc"] = build()
    return _BUILD_CACHE["nc"]


def kernel(**inputs):
    x = np.ascontiguousarray(np.asarray(inputs["L0_attn_u"], dtype=np.float32).reshape(ROWS, DM))
    consts = host_consts()
    com = {
        "lam": np.ascontiguousarray(np.asarray(inputs["L0_attn_Lambda_ri"], np.float32)),
        "p_ri": np.ascontiguousarray(np.asarray(inputs["L0_attn_p_ri"], np.float32)),
        "q_ri": np.ascontiguousarray(np.asarray(inputs["L0_attn_q_ri"], np.float32)),
        "vc": np.ascontiguousarray(np.asarray(inputs["L0_attn_Vc_ri"], np.float32)),
        "ct_in": np.ascontiguousarray(np.asarray(inputs["L0_attn_Ct_ri"], np.float32)),
        "bp": np.ascontiguousarray(np.asarray(inputs["L0_attn_Bp"], np.float32)),
        "lsr": np.ascontiguousarray(np.broadcast_to(
            np.asarray(inputs["L0_attn_log_step"], np.float32).reshape(1, 1), (64, 1))),
        "dd": np.ascontiguousarray(np.broadcast_to(
            np.asarray(inputs["L0_attn_D"], np.float32).reshape(1, 1), (128, 1))),
        "gqw": np.ascontiguousarray(np.asarray(inputs["L0_ffn_gate_qweight"], np.int32)),
        "gsc": np.ascontiguousarray(np.asarray(inputs["L0_ffn_gate_scales"], np.float32)),
        "gqz": np.ascontiguousarray(np.asarray(inputs["L0_ffn_gate_qzeros"], np.int32)),
        "uqw": np.ascontiguousarray(np.asarray(inputs["L0_ffn_up_qweight"], np.int32)),
        "usc": np.ascontiguousarray(np.asarray(inputs["L0_ffn_up_scales"], np.float32)),
        "uqz": np.ascontiguousarray(np.asarray(inputs["L0_ffn_up_qzeros"], np.int32)),
        "dqw": np.ascontiguousarray(np.asarray(inputs["L0_ffn_down_qweight"], np.int32)),
        "dsc": np.ascontiguousarray(np.asarray(inputs["L0_ffn_down_scales"], np.float32)),
        "dqz": np.ascontiguousarray(np.asarray(inputs["L0_ffn_down_qzeros"], np.int32)),
    }
    com.update(consts)
    in_maps = []
    for c in range(NCORE):
        m = dict(com)
        m["x"] = x[MROWS * c:MROWS * (c + 1), :]
        in_maps.append(m)
    nc = _get_nc()
    res = run_bass_kernel_spmd(nc, in_maps, core_ids=list(range(NCORE)))
    parts = [np.asarray(r["out"]) for r in res.results]
    return np.concatenate(parts, axis=0).reshape(B, S, DM)


# version 15 (final)
# speedup vs baseline: 1.1933x; 1.1933x over previous
import numpy as np
from contextlib import ExitStack

import concourse.bass as bass
import concourse.mybir as mybir
from concourse.bass_utils import run_bass_kernel_spmd

F32 = mybir.dt.float32
BF16 = mybir.dt.bfloat16
I32 = mybir.dt.int32
I16 = mybir.dt.int16
AF = mybir.ActivationFunctionType
ALU = mybir.AluOpType

NCORE = 8
B, S, DM = 2, 8192, 4096
ROWS = B * S
MROWS = ROWS // NCORE
NMT = MROWS // 128
IM = 11008
NNT = IM // 128
L = 4096
KT = DM // 128
BW = 8
BSTARTS = list(range(0, NNT, BW))
NB = len(BSTARTS)


class Sem:
    def __init__(self, handle):
        self.h = handle
        self.n = 0

    def inc(self, instr, k=1):
        instr.then_inc(self.h, k)
        self.n += k
        return instr


def host_consts():
    l = np.arange(L)
    Om = np.exp(-2j * np.pi * l / L)
    t_l = ((1 - Om) / (1 + Om)).imag
    c_l = 2.0 / (1 + Om)

    def to_pc(a):
        return np.ascontiguousarray(a.reshape(128, 32)).astype(np.float32)

    p0 = np.arange(128)
    t0g, cg = np.meshgrid(np.arange(128), np.arange(32), indexing="ij")
    CosM = np.cos(2 * np.pi * np.outer(p0, p0) / 128).astype(np.float32)
    SinM = np.sin(2 * np.pi * np.outer(p0, p0) / 128).astype(np.float32)
    c1 = np.arange(32)
    oneh = np.zeros((128, KT * KT), np.float32)
    for K in range(KT):
        oneh[:, KT * K + K] = 1.0
    return {
        "t_row": t_l.astype(np.float32).reshape(1, L),
        "cr_pc": to_pc(c_l.real), "ci_pc": to_pc(c_l.imag),
        "CosM": CosM, "SinM": SinM, "SinMneg": np.ascontiguousarray(-SinM),
        "Wtr": np.cos(2 * np.pi * cg * t0g / L).astype(np.float32),
        "Wti": np.sin(2 * np.pi * cg * t0g / L).astype(np.float32),
        "Er": (np.cos(2 * np.pi * np.outer(c1, c1) / 32) / L).astype(np.float32),
        "Ei": (np.sin(2 * np.pi * np.outer(c1, c1) / 32) / L).astype(np.float32),
        "Eineg": np.ascontiguousarray(-(np.sin(2 * np.pi * np.outer(c1, c1) / 32) / L)).astype(np.float32),
        "ident": np.eye(128, dtype=np.float32),
    }


def build():
    nc = bass.Bass("TRN2", target_bir_lowering=False)
    din = lambda n, s, d=F32: nc.dram_tensor(n, s, d, kind="ExternalInput").ap()

    x_in = din("x", [MROWS, DM])
    lam = din("lam", [64, 2]); p_ri = din("p_ri", [64, 2]); q_ri = din("q_ri", [64, 2])
    vc = din("vc", [64, 64, 2]); ct_in = din("ct_in", [1, 64, 2]); bp = din("bp", [64, 1])
    lsr = din("lsr", [64, 1]); dd_in = din("dd", [128, 1])
    gqw = din("gqw", [DM, IM // 8], I32); gsc = din("gsc", [32, IM]); gqz = din("gqz", [32, IM // 8], I32)
    uqw = din("uqw", [DM, IM // 8], I32); usc = din("usc", [32, IM]); uqz = din("uqz", [32, IM // 8], I32)
    dqw = din("dqw", [IM, DM // 8], I32); dsc = din("dsc", [86, DM]); dqz = din("dqz", [86, DM // 8], I32)
    t_row = din("t_row", [1, L])
    cr_pc = din("cr_pc", [128, 32]); ci_pc = din("ci_pc", [128, 32])
    CosM = din("CosM", [128, 128]); SinM = din("SinM", [128, 128]); SinMneg = din("SinMneg", [128, 128])
    Wtr = din("Wtr", [128, 32]); Wti = din("Wti", [128, 32])
    Er = din("Er", [32, 32]); Ei = din("Ei", [32, 32]); Eineg = din("Eineg", [32, 32])
    ident = din("ident", [128, 128])

    out = nc.dram_tensor("out", [MROWS, DM], F32, kind="ExternalOutput").ap()

    kscr = nc.dram_tensor("kscr", [8, L], F32, kind="Internal").ap()
    kpad = nc.dram_tensor("kpad", [1, 128 + L], BF16, kind="Internal").ap()
    ssc_g = nc.dram_tensor("ssc_g", [NNT, 32, 128], BF16, kind="Internal").ap()
    zsc_g = nc.dram_tensor("zsc_g", [NNT, 32, 128], BF16, kind="Internal").ap()
    ssc_u = nc.dram_tensor("ssc_u", [NNT, 32, 128], BF16, kind="Internal").ap()
    zsc_u = nc.dram_tensor("zsc_u", [NNT, 32, 128], BF16, kind="Internal").ap()
    ssc_d = nc.dram_tensor("ssc_d", [8, 86, 512], BF16, kind="Internal").ap()
    zsc_d = nc.dram_tensor("zsc_d", [8, 86, 512], BF16, kind="Internal").ap()
    hscr = nc.dram_tensor("hscr", [4, 128, NNT, 512], BF16, kind="Internal").ap()
    cscr = nc.dram_tensor("cscr", [MROWS, DM], F32, kind="Internal").ap()

    es0 = ExitStack()
    mk0 = es0.enter_context
    sb0 = lambda n, s, d=F32: mk0(nc.sbuf_tensor(n, s, d))

    LD = Sem(mk0(nc.semaphore("LD")))
    SLp = [Sem(mk0(nc.semaphore(f"SL{i}"))) for i in range(2)]
    DSS = [Sem(mk0(nc.semaphore(f"DSS{i}"))) for i in range(2)]
    XQp = [Sem(mk0(nc.semaphore(f"XQ{i}"))) for i in range(2)]
    XG = Sem(mk0(nc.semaphore("XG")))
    IQp = [Sem(mk0(nc.semaphore(f"IQ{i}"))) for i in range(2)]
    BCg = [Sem(mk0(nc.semaphore(f"BCg{i}"))) for i in range(4)]
    BCu = [Sem(mk0(nc.semaphore(f"BCu{i}"))) for i in range(4)]
    BCd = [Sem(mk0(nc.semaphore(f"BCd{i}"))) for i in range(2)]
    HSp = [Sem(mk0(nc.semaphore(f"HS{i}"))) for i in range(2)]
    CVS = Sem(mk0(nc.semaphore("CVS")))
    HLp = [Sem(mk0(nc.semaphore(f"HL{i}"))) for i in range(8)]
    SOp = [Sem(mk0(nc.semaphore(f"SO{i}"))) for i in range(2)]
    DS = Sem(mk0(nc.semaphore("DS")))
    TB = Sem(mk0(nc.semaphore("TB")))
    CV = Sem(mk0(nc.semaphore("CV")))
    CS = Sem(mk0(nc.semaphore("CS")))
    CT = Sem(mk0(nc.semaphore("CT")))

    s_dd = sb0("s_dd", [128, 1])
    s_rstd = sb0("s_rstd", [128, NMT])
    s_nbias = sb0("s_nbias", [128, NMT])
    s_id = sb0("s_id", [128, 128])
    s_idbf = sb0("s_idbf", [128, 128], BF16)

    es_xnT = ExitStack()
    s_xnT = es_xnT.enter_context(nc.sbuf_tensor("s_xnT", [128, KT * MROWS], BF16))

    es_T = ExitStack()
    s_T = es_T.enter_context(nc.sbuf_tensor("s_T", [128, 32 * 128], BF16))

    marks = {}

    with es0, nc.Block() as block:
        es1 = ExitStack()
        mk1 = es1.enter_context
        sb1 = lambda n, s, d=F32: mk1(nc.sbuf_tensor(n, s, d))

        s_lam = sb1("s_lam", [64, 2]); s_p = sb1("s_p", [64, 2]); s_q = sb1("s_q", [64, 2])
        s_vc = sb1("s_vc", [64, 128]); s_vct = sb1("s_vct", [64, 128])
        s_ct = sb1("s_ct", [64, 2]); s_bp = sb1("s_bp", [64, 1]); s_b = sb1("s_b", [64, 2])
        s_lsr = sb1("s_lsr", [64, 1]); s_step = sb1("s_step", [64, 1]); s_s2 = sb1("s_s2", [64, 1])
        s_lr2 = sb1("s_lr2", [64, 1]); s_lrneg = sb1("s_lrneg", [64, 1])
        s_vP = sb1("s_vP", [64, 8]); s_vQ = sb1("s_vQ", [64, 8]); s_vt = sb1("s_vt", [64, 8])
        s_cr = sb1("s_cr", [128, 32]); s_ci = sb1("s_ci", [128, 32])
        s_Wtr = sb1("s_Wtr", [128, 32]); s_Wti = sb1("s_Wti", [128, 32])
        s_CosM = sb1("s_CosM", [128, 128]); s_SinM = sb1("s_SinM", [128, 128])
        s_SinMneg = sb1("s_SinMneg", [128, 128])
        s_Er = sb1("s_Er", [32, 32]); s_Ei = sb1("s_Ei", [32, 32]); s_Eineg = sb1("s_Eineg", [32, 32])
        kk = [sb1(f"kk{i}", [128, 32]) for i in range(8)]
        cw = [sb1(f"cw{i}", [128, 32]) for i in range(10)]
        s_atr = sb1("s_atr", [128, 32]); s_ati = sb1("s_ati", [128, 32])
        s_Fr = sb1("s_Fr", [128, 32]); s_Fi = sb1("s_Fi", [128, 32])
        s_Gr = sb1("s_Gr", [128, 32]); s_Gi = sb1("s_Gi", [128, 32])
        s_GrT = sb1("s_GrT", [32, 128]); s_GiT = sb1("s_GiT", [32, 128])
        s_K = sb1("s_K", [32, 128]); s_Kbf = sb1("s_Kbf", [32, 128], BF16)
        s_zrow = sb1("s_zrow", [1, 128], BF16)

        pP0 = mk1(nc.psum_tensor("pP0", [128, 512], F32))

        es_K = ExitStack()
        mkK = es_K.enter_context
        s_tch = mkK(nc.sbuf_tensor("s_tch", [64, 512], F32))
        s_di = mkK(nc.sbuf_tensor("s_di", [64, 512], F32))
        s_den = mkK(nc.sbuf_tensor("s_den", [64, 512], F32))
        s_rec = mkK(nc.sbuf_tensor("s_rec", [64, 512], F32))
        s_P = mkK(nc.sbuf_tensor("s_P", [64, 512], F32))
        s_Q = mkK(nc.sbuf_tensor("s_Q", [64, 512], F32))

        SCW = 1024
        es_big = ExitStack()
        mkbig = es_big.enter_context
        s_scf = [mkbig(nc.sbuf_tensor(f"s_scf{b}", [86, SCW], F32)) for b in range(2)]
        s_znib = [mkbig(nc.sbuf_tensor(f"s_znib{b}", [86, SCW], I16)) for b in range(2)]
        s_zw = [mkbig(nc.sbuf_tensor(f"s_zw{b}", [86, SCW // 8], I32)) for b in range(2)]
        s_scb = [mkbig(nc.sbuf_tensor(f"s_scb{b}", [86, SCW], BF16)) for b in range(2)]
        s_k8 = mkbig(nc.sbuf_tensor("s_k8", [8, L], F32))

        @block.sync
        def _(sync):
            for dst, src in [
                (s_lam[:], lam), (s_p[:], p_ri), (s_q[:], q_ri),
                (s_vc[:], vc.rearrange("m k c -> m (k c)")),
                (s_ct[:], ct_in.rearrange("o n c -> (o n) c")),
                (s_bp[:], bp), (s_lsr[:], lsr), (s_dd[:], dd_in),
                (s_cr[:], cr_pc), (s_ci[:], ci_pc),
                (s_CosM[:], CosM), (s_SinM[:], SinM), (s_SinMneg[:], SinMneg),
                (s_Wtr[:], Wtr), (s_Wti[:], Wti),
                (s_Er[:], Er), (s_Ei[:], Ei), (s_Eineg[:], Eineg),
                (s_id[:], ident),
            ]:
                LD.inc(sync.dma_start(out=dst, in_=src), 16)
        ld_consts = LD.n

        @block.vector
        def _(vector):
            vector.wait_ge(LD.h, ld_consts)
            CV.inc(vector.tensor_copy(out=s_idbf[:], in_=s_id[:]))
            marks["idbf"] = CV.n
            vector.wait_ge(CV.h, CV.n)

        stage_jobs = []

        def stage_one(tag, sc_dram, qz_dram, s_out, z_out, G, N, zs_mode):
            for c0 in range(0, N, SCW):
                cw_ = min(SCW, N - c0)
                stage_jobs.append((tag, sc_dram, qz_dram, s_out, z_out, G, N, zs_mode, c0, cw_))

        stage_ctr = [0]

        def emit_stage_chunk(job):
            (tag, sc_dram, qz_dram, s_out, z_out, G, N, zs_mode, c0, cw_) = job
            if True:
                ci = stage_ctr[0]
                stage_ctr[0] += 1
                pa = ci % 2
                scf, znib, zw, scb = s_scf[pa], s_znib[pa], s_zw[pa], s_scb[pa]

                @block.sync
                def _(sync):
                    if ci >= 2:
                        sync.wait_ge(DSS[pa].h, DSS[pa].n)
                    SLp[pa].inc(sync.dma_start(out=scf[0:G, 0:cw_], in_=sc_dram[:, c0:c0 + cw_]), 16)
                    SLp[pa].inc(sync.dma_start(out=zw[0:G, 0:cw_ // 8],
                                          in_=qz_dram[:, c0 // 8:(c0 + cw_) // 8]), 16)
                ldv = SLp[pa].n

                @block.vector
                def _(vector):
                    vector.wait_ge(SLp[pa].h, ldv)
                    CV.inc(vector.tensor_copy(out=scb[0:G, 0:cw_], in_=scf[0:G, 0:cw_]))
                    vector.wait_ge(CV.h, CV.n)
                cv_s = CV.n

                @block.sync
                def _(sync):
                    sync.wait_ge(CV.h, cv_s)
                    if G == 32:
                        DSS[pa].inc(sync.dma_start(
                            out=s_out[c0 // 128:(c0 + cw_) // 128, :, :].transpose([1, 0, 2]),
                            in_=scb[0:G, 0:cw_].rearrange("g (t n) -> g t n", n=128)), 16)
                    else:
                        DSS[pa].inc(sync.dma_start(
                            out=s_out[c0 // 512:(c0 + cw_) // 512, :, :].transpose([1, 0, 2]),
                            in_=scb[0:G, 0:cw_].rearrange("g (t n) -> g t n", n=512)), 16)
                ds_s = DSS[pa].n

                @block.vector
                def _(vector):
                    zv16 = zw[0:G, 0:cw_ // 8].bitcast(I16).rearrange("p (w h) -> p w h", h=2)
                    nv16 = znib[0:G, 0:cw_].rearrange("p (w e) -> p w e", e=8)
                    for sh in range(4):
                        CV.inc(vector.tensor_scalar(
                            out=nv16[:, :, 2 * sh:2 * sh + 2], in0=zv16,
                            scalar1=(4 * sh), scalar2=15,
                            op0=ALU.logical_shift_right, op1=ALU.bitwise_and))
                    vector.wait_ge(CV.h, CV.n)
                    vector.wait_ge(DSS[pa].h, ds_s)
                    if zs_mode:
                        CV.inc(vector.tensor_tensor(out=scb[0:G, 0:cw_], in0=znib[0:G, 0:cw_],
                                                    in1=scf[0:G, 0:cw_], op=ALU.mult))
                    else:
                        CV.inc(vector.tensor_copy(out=scb[0:G, 0:cw_], in_=znib[0:G, 0:cw_]))
                    vector.wait_ge(CV.h, CV.n)
                cv_z = CV.n

                @block.sync
                def _(sync):
                    sync.wait_ge(CV.h, cv_z)
                    if G == 32:
                        DSS[pa].inc(sync.dma_start(
                            out=z_out[c0 // 128:(c0 + cw_) // 128, :, :].transpose([1, 0, 2]),
                            in_=scb[0:G, 0:cw_].rearrange("g (t n) -> g t n", n=128)), 16)
                    else:
                        DSS[pa].inc(sync.dma_start(
                            out=z_out[c0 // 512:(c0 + cw_) // 512, :, :].transpose([1, 0, 2]),
                            in_=scb[0:G, 0:cw_].rearrange("g (t n) -> g t n", n=512)), 16)

        stage_one("g", gsc, gqz, ssc_g, zsc_g, 32, IM, False)
        stage_one("u", usc, uqz, ssc_u, zsc_u, 32, IM, False)
        stage_one("d", dsc, dqz, ssc_d, zsc_d, 86, DM, False)

        @block.scalar
        def _(scalar):
            scalar.wait_ge(LD.h, ld_consts)
            CS.inc(scalar.activation(out=s_step[:], in_=s_lsr[:], func=AF.Exp))
        cs_step = CS.n

        @block.vector
        def _(vector):
            vector.wait_ge(CS.h, cs_step)
            CV.inc(vector.reciprocal(out=s_s2[:], in_=s_step[:]))
            vector.wait_ge(CV.h, CV.n)
            CV.inc(vector.tensor_scalar(out=s_s2[:], in0=s_s2[:], scalar1=2.0, scalar2=None, op0=ALU.mult))
            CV.inc(vector.tensor_tensor(out=s_lr2[:], in0=s_lam[:, 0:1], in1=s_lam[:, 0:1], op=ALU.mult))
            CV.inc(vector.tensor_scalar(out=s_lrneg[:], in0=s_lam[:, 0:1], scalar1=-1.0, scalar2=None,
                                        op0=ALU.mult))
            vector.wait_ge(CV.h, CV.n)

        @block.tensor
        def _(tensor):
            tensor.wait_ge(LD.h, ld_consts)
            vcr = s_vc[:].rearrange("m (k c) -> m k c", c=2)
            tensor.transpose(out=pP0[0:64, 0:64], in_=vcr[:, :, 0], identity=s_id[0:64, 0:64])
            CT.inc(tensor.transpose(out=pP0[0:64, 64:128], in_=vcr[:, :, 1], identity=s_id[0:64, 0:64]))
        ct_vc = CT.n

        @block.scalar
        def _(scalar):
            scalar.wait_ge(CT.h, ct_vc)
            CS.inc(scalar.copy(out=s_vct[:], in_=pP0[0:64, 0:128]))
        cs_vct = CS.n

        @block.tensor
        def _(tensor):
            tensor.wait_ge(CS.h, cs_vct)
            tensor.matmul(out=pP0[0:64, 0:1], lhsT=s_vct[:, 0:64], rhs=s_bp[:], start=True, stop=True)
            CT.inc(tensor.matmul(out=pP0[0:64, 1:2], lhsT=s_vct[:, 64:128], rhs=s_bp[:], start=True, stop=True))
        ct_b = CT.n

        @block.scalar
        def _(scalar):
            scalar.wait_ge(CT.h, ct_b)
            CS.inc(scalar.copy(out=s_b[:], in_=pP0[0:64, 0:2]))
        cs_b = CS.n

        @block.vector
        def _(vector):
            vector.wait_ge(CS.h, cs_b)
            pairs = [(s_ct, s_b), (s_ct, s_p), (s_q, s_b), (s_q, s_p)]
            for j, (a, b2) in enumerate(pairs):
                ar, ai, br_, bi = a[:, 0:1], a[:, 1:2], b2[:, 0:1], b2[:, 1:2]
                CV.inc(vector.tensor_tensor(out=s_vt[:, 0:1], in0=ar, in1=br_, op=ALU.mult))
                CV.inc(vector.tensor_tensor(out=s_vt[:, 1:2], in0=ai, in1=bi, op=ALU.mult))
                CV.inc(vector.tensor_tensor(out=s_vt[:, 2:3], in0=ar, in1=bi, op=ALU.mult))
                CV.inc(vector.tensor_tensor(out=s_vt[:, 3:4], in0=ai, in1=br_, op=ALU.mult))
                vector.wait_ge(CV.h, CV.n)
                CV.inc(vector.tensor_tensor(out=s_vP[:, j:j + 1], in0=s_vt[:, 0:1], in1=s_vt[:, 1:2],
                                            op=ALU.add))
                CV.inc(vector.tensor_tensor(out=s_vP[:, 4 + j:5 + j], in0=s_vt[:, 2:3], in1=s_vt[:, 3:4],
                                            op=ALU.subtract))
                vector.wait_ge(CV.h, CV.n)
            CV.inc(vector.tensor_copy(out=s_vQ[:, 0:4], in_=s_vP[:, 4:8]))
            CV.inc(vector.tensor_scalar(out=s_vQ[:, 4:8], in0=s_vP[:, 0:4], scalar1=-1.0, scalar2=None,
                                        op0=ALU.mult))
        marks["cv_v"] = CV.n

        def emit_K_chunk(ch):
            @block.sync
            def _(sync):
                if ch == 0:
                    sync.wait_ge(CV.h, marks["cv_v"])
                else:
                    sync.wait_ge(CV.h, marks[f"pq{ch-1}"])
                LD.inc(sync.dma_start(out=s_tch[:],
                                      in_=t_row[0:1, 512 * ch:512 * (ch + 1)].to_broadcast((64, 512))), 16)
            ld_t = LD.n

            @block.vector
            def _(vector):
                vector.wait_ge(LD.h, ld_t)
                if ch:
                    vector.wait_ge(CS.h, marks[f"kev{ch-1}"])
                CV.inc(vector.tensor_scalar(out=s_di[:], in0=s_tch[:], scalar1=s_s2[:],
                                            scalar2=s_lam[:, 1:2], op0=ALU.mult, op1=ALU.subtract))
                vector.wait_ge(CV.h, CV.n)
                CV.inc(vector.tensor_tensor(out=s_den[:], in0=s_di[:], in1=s_di[:], op=ALU.mult))
                vector.wait_ge(CV.h, CV.n)
                CV.inc(vector.tensor_scalar(out=s_den[:], in0=s_den[:], scalar1=s_lr2[:], scalar2=None,
                                            op0=ALU.add))
                vector.wait_ge(CV.h, CV.n)
                CV.inc(vector.reciprocal(out=s_rec[:], in_=s_den[:]))
                vector.wait_ge(CV.h, CV.n)
                CV.inc(vector.tensor_scalar(out=s_P[:], in0=s_rec[:], scalar1=s_lrneg[:], scalar2=None,
                                            op0=ALU.mult))
                CV.inc(vector.tensor_tensor(out=s_Q[:], in0=s_di[:], in1=s_rec[:], op=ALU.mult))
                marks[f"pq{ch}"] = CV.n
                vector.wait_ge(CV.h, CV.n)

            @block.tensor
            def _(tensor):
                tensor.wait_ge(CV.h, marks[f"pq{ch}"])
                if ch:
                    tensor.wait_ge(CS.h, marks[f"kev{ch-1}"])
                tensor.matmul(out=pP0[0:8, 0:512], lhsT=s_vP[:], rhs=s_P[:], start=True, stop=False)
                CT.inc(tensor.matmul(out=pP0[0:8, 0:512], lhsT=s_vQ[:], rhs=s_Q[:], start=False, stop=True))
            ct_ch = CT.n

            @block.scalar
            def _(scalar):
                scalar.wait_ge(CT.h, ct_ch)
                CS.inc(scalar.copy(out=s_k8[:, 512 * ch:512 * (ch + 1)], in_=pP0[0:8, 0:512]))
                marks[f"kev{ch}"] = CS.n

        for j in range(max(len(stage_jobs), 8)):
            if j < len(stage_jobs):
                emit_stage_chunk(stage_jobs[j])
            if j < 8:
                emit_K_chunk(j)
        dss_done = (DSS[0].n, DSS[1].n)
        cs_k8 = CS.n

        @block.sync
        def _(sync):
            sync.wait_ge(CS.h, cs_k8)
            DS.inc(sync.dma_start(out=kscr, in_=s_k8[:]), 16)
            sync.wait_ge(DS.h, DS.n)
            for i in range(8):
                DS.inc(sync.dma_start(out=kk[i][:],
                                      in_=kscr[i:i + 1, :].rearrange("o (p c) -> (o p) c", p=128)), 16)
        ds_kk = DS.n
        es_big.close()
        es_K.close()

        @block.vector
        def _(vector):
            vector.wait_ge(DS.h, ds_kk)
            kr00, kr01, kr10, kr11 = kk[0][:], kk[1][:], kk[2][:], kk[3][:]
            ki00, ki01, ki10, ki11 = kk[4][:], kk[5][:], kk[6][:], kk[7][:]
            d1, num_r, num_i, den2, rden, zr, zi, wr_, wi_, t9 = [c[:] for c in cw]

            def tt(o, a, b2, op):
                CV.inc(vector.tensor_tensor(out=o, in0=a, in1=b2, op=op))

            def w():
                vector.wait_ge(CV.h, CV.n)

            CV.inc(vector.tensor_scalar(out=d1, in0=kr11, scalar1=1.0, scalar2=None, op0=ALU.add))
            tt(num_r, kr01, kr10, ALU.mult); tt(t9, ki01, ki10, ALU.mult); w()
            tt(num_r, num_r, t9, ALU.subtract)
            tt(num_i, kr01, ki10, ALU.mult); w(); tt(t9, ki01, kr10, ALU.mult); w()
            tt(num_i, num_i, t9, ALU.add)
            tt(den2, d1, d1, ALU.mult); w(); tt(t9, ki11, ki11, ALU.mult); w()
            tt(den2, den2, t9, ALU.add); w()
            CV.inc(vector.reciprocal(out=rden, in_=den2)); w()
            tt(zr, num_r, d1, ALU.mult); tt(t9, num_i, ki11, ALU.mult); w()
            tt(zr, zr, t9, ALU.add)
            tt(zi, num_i, d1, ALU.mult); w(); tt(t9, num_r, ki11, ALU.mult); w()
            tt(zi, zi, t9, ALU.subtract); w()
            tt(zr, zr, rden, ALU.mult); tt(zi, zi, rden, ALU.mult); w()
            tt(wr_, kr00, zr, ALU.subtract); tt(wi_, ki00, zi, ALU.subtract); w()
            tt(s_atr[:], s_cr[:], wr_, ALU.mult); tt(t9, s_ci[:], wi_, ALU.mult); w()
            tt(s_atr[:], s_atr[:], t9, ALU.subtract)
            tt(s_ati[:], s_cr[:], wi_, ALU.mult); w(); tt(t9, s_ci[:], wr_, ALU.mult); w()
            tt(s_ati[:], s_ati[:], t9, ALU.add); w()
        cv_at = CV.n

        @block.tensor
        def _(tensor):
            tensor.wait_ge(CV.h, cv_at)
            tensor.matmul(out=pP0[:, 0:32], lhsT=s_CosM[:], rhs=s_atr[:], start=True, stop=False)
            tensor.matmul(out=pP0[:, 0:32], lhsT=s_SinMneg[:], rhs=s_ati[:], start=False, stop=True)
            tensor.matmul(out=pP0[:, 32:64], lhsT=s_CosM[:], rhs=s_ati[:], start=True, stop=False)
            CT.inc(tensor.matmul(out=pP0[:, 32:64], lhsT=s_SinM[:], rhs=s_atr[:], start=False, stop=True))
        ct_f = CT.n

        @block.scalar
        def _(scalar):
            scalar.wait_ge(CT.h, ct_f)
            CS.inc(scalar.copy(out=s_Fr[:], in_=pP0[:, 0:32]))
            CS.inc(scalar.copy(out=s_Fi[:], in_=pP0[:, 32:64]))
        cs_f = CS.n

        @block.vector
        def _(vector):
            def tt(o, a, b2, op):
                CV.inc(vector.tensor_tensor(out=o, in0=a, in1=b2, op=op))

            vector.wait_ge(CS.h, cs_f)
            tt(s_Gr[:], s_Fr[:], s_Wtr[:], ALU.mult); tt(cw[9][:], s_Fi[:], s_Wti[:], ALU.mult)
            vector.wait_ge(CV.h, CV.n)
            tt(s_Gr[:], s_Gr[:], cw[9][:], ALU.subtract)
            tt(s_Gi[:], s_Fr[:], s_Wti[:], ALU.mult)
            vector.wait_ge(CV.h, CV.n)
            tt(cw[9][:], s_Fi[:], s_Wtr[:], ALU.mult)
            vector.wait_ge(CV.h, CV.n)
            tt(s_Gi[:], s_Gi[:], cw[9][:], ALU.add)
            vector.wait_ge(CV.h, CV.n)
        cv_g = CV.n

        @block.tensor
        def _(tensor):
            tensor.wait_ge(CV.h, cv_g)
            tensor.transpose(out=pP0[0:32, 0:128], in_=s_Gr[:], identity=s_id[:])
            CT.inc(tensor.transpose(out=pP0[0:32, 128:256], in_=s_Gi[:], identity=s_id[:]))
        ct_gt = CT.n

        @block.scalar
        def _(scalar):
            scalar.wait_ge(CT.h, ct_gt)
            CS.inc(scalar.copy(out=s_GrT[:], in_=pP0[0:32, 0:128]))
            CS.inc(scalar.copy(out=s_GiT[:], in_=pP0[0:32, 128:256]))
        cs_gt = CS.n

        @block.tensor
        def _(tensor):
            tensor.wait_ge(CS.h, cs_gt)
            tensor.matmul(out=pP0[0:32, 0:128], lhsT=s_Er[:], rhs=s_GrT[:], start=True, stop=False)
            CT.inc(tensor.matmul(out=pP0[0:32, 0:128], lhsT=s_Eineg[:], rhs=s_GiT[:], start=False, stop=True))
        ct_k = CT.n

        @block.scalar
        def _(scalar):
            scalar.wait_ge(CT.h, ct_k)
            CS.inc(scalar.copy(out=s_K[:], in_=pP0[0:32, 0:128]))
        cs_kk2 = CS.n

        @block.vector
        def _(vector):
            vector.wait_ge(CS.h, cs_kk2)
            CV.inc(vector.tensor_copy(out=s_Kbf[:], in_=s_K[:]))
            CV.inc(vector.memset(s_zrow[:], 0.0))
            vector.wait_ge(CV.h, CV.n)
        cv_kbf = CV.n

        @block.sync
        def _(sync):
            sync.wait_ge(CV.h, cv_kbf)
            DS.inc(sync.dma_start(out=kpad[0:1, 0:128], in_=s_zrow[:]), 16)
            DS.inc(sync.dma_start(out=kpad[0:1, 128:128 + L].rearrange("o (a b) -> (o a) b", a=32),
                                  in_=s_Kbf[:]), 16)
        ds_kpad = DS.n

        @block.gpsimd
        def _(gpsimd):
            gpsimd.wait_ge(DS.h, ds_kpad)
            for r in range(128):
                TB.inc(gpsimd.dma_start(out=s_T[r:r + 1, :], in_=kpad[0:1, 128 - r:128 - r + L]), 16)
        ds_T = TB.n

        es_p1 = ExitStack()
        mkp1 = es_p1.enter_context
        s_x = [mkp1(nc.sbuf_tensor(f"s_x{b}", [128, DM], F32)) for b in range(2)]
        s_xnb = [mkp1(nc.sbuf_tensor(f"s_xnb{b}", [128, DM], BF16)) for b in range(2)]
        s_st6 = [mkp1(nc.sbuf_tensor(f"s_st6{b}", [128, 48], F32)) for b in range(2)]
        s_mv = [mkp1(nc.sbuf_tensor(f"s_mv{b}", [128, 2], F32)) for b in range(2)]
        s_v1 = [mkp1(nc.sbuf_tensor(f"s_v1{b}", [128, 1], F32)) for b in range(2)]
        s_v2 = [mkp1(nc.sbuf_tensor(f"s_v2{b}", [128, 1], F32)) for b in range(2)]
        s_cv = mkp1(nc.sbuf_tensor("s_cv", [128, 2048], F32))

        ptr_ps = mkp1(nc.psum_tensor("ptr_ps", [128, 2048], BF16))
        pcv = mkp1(nc.psum_tensor("pcv", [128, 2048], F32))

        xl_mark = [0] * NMT
        xn_mark = [0] * NMT

        def emit_conv(mt):
            for half in range(2):
                @block.tensor
                def _(tensor):
                    tensor.wait_ge(CV.h, marks[(mt, 1)])
                    if mt == 0 and half == 0:
                        tensor.wait_ge(TB.h, ds_T)
                    ckey = ("cv", mt, half - 1) if half else ("cv", mt - 1, 1)
                    if ckey in marks:
                        tensor.wait_ge(CV.h, marks[ckey])
                    ilo_h = 16 * half
                    args = []
                    for j_blk in range(ilo_h + 16):
                        for bl in range(4):
                            i_lo = max(ilo_h + 4 * bl, j_blk)
                            i_hi = ilo_h + 4 * bl + 4
                            if i_lo < i_hi:
                                args.append((j_blk, i_lo, i_hi))
                    for idx, (j_blk, i_lo, i_hi) in enumerate(args):
                        d_lo = i_lo - j_blk
                        mm = tensor.matmul(
                            out=pcv[:, (i_lo - ilo_h) * 128:(i_hi - ilo_h) * 128],
                            lhsT=s_xnT[:, MROWS * j_blk + 128 * mt: MROWS * j_blk + 128 * (mt + 1)],
                            rhs=s_T[:, 128 * d_lo:128 * (d_lo + (i_hi - i_lo))],
                            start=(j_blk == 0), stop=(j_blk == i_hi - 1))
                        if idx == len(args) - 1:
                            CT.inc(mm)
                marks[("cvt", mt, half)] = CT.n

                @block.vector
                def _(vector):
                    vector.wait_ge(CT.h, marks[("cvt", mt, half)])
                    if mt or half:
                        vector.wait_ge(CVS.h, CVS.n)
                    CV.inc(vector.tensor_copy(out=s_cv[:], in_=pcv[:]))
                    vector.wait_ge(CV.h, CV.n)
                marks[("cv", mt, half)] = CV.n

                @block.gpsimd
                def _(gpsimd):
                    gpsimd.wait_ge(CV.h, marks[("cv", mt, half)])
                    CVS.inc(gpsimd.dma_start(
                        out=cscr[128 * mt:128 * (mt + 1), 2048 * half:2048 * (half + 1)],
                        in_=s_cv[:]), 16)

        def emit_load(mt):
            @block.sync
            def _(sync):
                if mt == 0:
                    sync.wait_ge(CS.h, cs_k8)
                    sync.wait_ge(DS.h, ds_kk)
                    sync.wait_ge(DSS[0].h, dss_done[0])
                    sync.wait_ge(DSS[1].h, dss_done[1])
                if mt >= 2:
                    sync.wait_ge(CV.h, xn_mark[mt - 2])
                XQp[mt % 2].inc(sync.dma_start(out=s_x[mt % 2][:], in_=x_in[128 * mt:128 * (mt + 1), :]), 16)
                xl_mark[mt] = XQp[mt % 2].n

        def emit_lnx(mt):
            @block.vector
            def _(vector):
                vector.wait_ge(XQp[mt % 2].h, xl_mark[mt])
                if mt >= 2:
                    vector.wait_ge(CV.h, marks[f"rstd{mt-2}"])
                for c8 in range(8):
                    CV.inc(vector.bn_stats(out=s_st6[mt % 2][:, 6 * c8:6 * (c8 + 1)],
                                           in_=s_x[mt % 2][:, 512 * c8:512 * (c8 + 1)]))
                vector.wait_ge(CV.h, CV.n)
                CV.inc(vector.bn_aggr(out=s_mv[mt % 2][:], in_=s_st6[mt % 2][:]))
                vector.wait_ge(CV.h, CV.n)
                CV.inc(vector.tensor_scalar(out=s_v1[mt % 2][:], in0=s_mv[mt % 2][:, 1:2], scalar1=1e-5,
                                            scalar2=None, op0=ALU.add))
                vector.wait_ge(CV.h, CV.n)
                CV.inc(vector.reciprocal(out=s_v2[mt % 2][:], in_=s_v1[mt % 2][:]))
                vector.wait_ge(CV.h, CV.n)
            marks[f"ln{mt}"] = CV.n

            @block.scalar
            def _(scalar):
                scalar.wait_ge(CV.h, marks[f"ln{mt}"])
                CS.inc(scalar.activation(out=s_rstd[:, mt:mt + 1], in_=s_v2[mt % 2][:], func=AF.Sqrt))
            marks[f"sq{mt}"] = CS.n

            @block.vector
            def _(vector):
                vector.wait_ge(CS.h, marks[f"sq{mt}"])
                CV.inc(vector.tensor_tensor(out=s_v1[mt % 2][:], in0=s_mv[mt % 2][:, 0:1],
                                            in1=s_rstd[:, mt:mt + 1], op=ALU.mult))
                vector.wait_ge(CV.h, CV.n)
                CV.inc(vector.tensor_scalar(out=s_nbias[:, mt:mt + 1], in0=s_v1[mt % 2][:], scalar1=-1.0,
                                            scalar2=None, op0=ALU.mult))
                vector.wait_ge(CV.h, CV.n)
                if (mt - 2, 1, "t") in marks:
                    vector.wait_ge(CT.h, marks[(mt - 2, 1, "t")])
                CV.inc(vector.tensor_scalar(out=s_xnb[mt % 2][:], in0=s_x[mt % 2][:],
                                            scalar1=s_rstd[:, mt:mt + 1],
                                            scalar2=s_nbias[:, mt:mt + 1],
                                            op0=ALU.mult, op1=ALU.add))
                marks[f"rstd{mt}"] = CV.n
                xn_mark[mt] = CV.n
                vector.wait_ge(CV.h, CV.n)
            marks[f"nb{mt}"] = CV.n

        def emit_trans(mt):
            for rnd in range(2):
                @block.tensor
                def _(tensor):
                    tensor.wait_ge(CV.h, xn_mark[mt])
                    if mt == 0 and rnd == 0:
                        tensor.wait_ge(CV.h, marks["idbf"])
                    key = (mt, rnd - 1) if rnd else (mt - 1, 1)
                    if key in marks:
                        tensor.wait_ge(CV.h, marks[key])
                    for ft16 in range(16):
                        ft = 16 * rnd + ft16
                        mm = tensor.transpose(out=ptr_ps[:, 128 * ft16:128 * (ft16 + 1)],
                                              in_=s_xnb[mt % 2][:, 128 * ft:128 * (ft + 1)],
                                              identity=s_idbf[:])
                        if ft16 == 15:
                            CT.inc(mm)
                marks[(mt, rnd, "t")] = CT.n

                @block.vector
                def _(vector):
                    vector.wait_ge(CT.h, marks[(mt, rnd, "t")])
                    xv = s_xnT[:].rearrange("p (K m) -> p K m", m=MROWS)
                    CV.inc(vector.tensor_copy(
                        out=xv[:, 16 * rnd:16 * (rnd + 1), 128 * mt:128 * (mt + 1)],
                        in_=ptr_ps[:].rearrange("p (K c) -> p K c", c=128)))
                    vector.wait_ge(CV.h, CV.n)
                marks[(mt, rnd)] = CV.n

        emit_load(0)
        emit_load(1)
        emit_lnx(0)
        for k in range(NMT + 2):
            if k < NMT:
                emit_trans(k)
            if k + 1 < NMT:
                emit_lnx(k + 1)
            if 0 <= k - 2 < NMT:
                emit_conv(k - 2)
            if k + 2 < NMT:
                emit_load(k + 2)

        cs_p1 = CS.n
        cv_p1 = CV.n
        cvs_conv = CVS.n

        es_p1.close()
        es1.close()
        es_T.close()

        esA = ExitStack()
        mkA = esA.enter_context
        sbA = lambda n, s, d=F32: mkA(nc.sbuf_tensor(n, s, d))

        ig = [sbA(f"ig{b}", [128, KT * 16], I32) for b in range(2)]
        iu = [sbA(f"iu{b}", [128, KT * 16], I32) for b in range(2)]
        wg = [sbA(f"wg{b}", [128, KT * 128], BF16) for b in range(2)]
        wu = [sbA(f"wu{b}", [128, KT * 128], BF16) for b in range(2)]
        reps = {(ty, ma, qq): sbA(f"rep_{ty}{ma}{qq}", [128, 1024], BF16)
                for ty in "sz" for ma in "gu" for qq in range(3)}
        s_nib = sbA("s_nib", [128, 1024], I16)
        s_tmpb = sbA("s_tmpb", [128, 1024], BF16)
        hb = [sbA(f"hb{b}", [128, MROWS], BF16) for b in range(2)]

        pg = mkA(nc.psum_tensor("pg", [128, MROWS], F32))
        pu = mkA(nc.psum_tensor("pu", [128, MROWS], F32))

        iq_mark = {}
        bc_mark = {}
        vd_mark = {}
        vd_mark_q = {}
        exd_mark = {}
        tg_mark = [0] * (NNT + 1)
        tu_mark = [0] * (NNT + 1)
        silu_mark = [0] * NNT
        hmul_mark = [0] * NNT
        hspill_mark = [0] * NNT
        first_bc = [True]

        def emit_A_loads(t):
            @block.sync
            def _(sync):
                if t == 0:
                    sync.wait_ge(CS.h, cs_p1)
                    sync.wait_ge(CV.h, cv_p1)
                if t >= 2:
                    sync.wait_ge(CV.h, exd_mark[t - 2])
                for mat, src, buf in [("g", gqw, ig[t % 2]), ("u", uqw, iu[t % 2])]:
                    v = src.rearrange("(K p) w -> p K w", p=128)[:, :, 16 * t:16 * (t + 1)]
                    dv = buf[:].rearrange("p (K w) -> p K w", w=16)
                    IQp[t % 2].inc(sync.dma_start(out=dv[:, 0:16, :], in_=v[:, 0:16, :]), 16)
                    IQp[t % 2].inc(sync.dma_start(out=dv[:, 16:32, :], in_=v[:, 16:32, :]), 16)
                iq_mark[t] = IQp[t % 2].n

        def emit_one_bc(scalar, t, mat, q):
            sscr, zscr = {"g": (ssc_g, zsc_g), "u": (ssc_u, zsc_u)}[mat]
            BCm = {"g": BCg, "u": BCu}[mat][q % 3]
            if q >= 3:
                prev = (t, mat, 0)
            elif q == 0:
                prev = (t - 1, mat, 3)
            else:
                prev = (t - 1, mat, q)
            if prev in vd_mark_q:
                scalar.wait_ge(CV.h, vd_mark_q[prev])
            BCm.inc(scalar.dma_start(
                out=reps[("s", mat, q % 3)][:].rearrange("p (g n) -> p g n", g=8),
                in_=sscr[t, 8 * q:8 * (q + 1), :].partition_broadcast(128)), 16)
            BCm.inc(scalar.dma_start(
                out=reps[("z", mat, q % 3)][:].rearrange("p (g n) -> p g n", g=8),
                in_=zscr[t, 8 * q:8 * (q + 1), :].partition_broadcast(128)), 16)
            bc_mark[(t, mat, q)] = BCm.n

        def emit_A_bc(t):
            @block.scalar
            def _(scalar):
                if first_bc[0]:
                    scalar.wait_ge(DS.h, ds_kpad)
                    scalar.wait_ge(DSS[0].h, dss_done[0])
                    scalar.wait_ge(DSS[1].h, dss_done[1])
                    scalar.wait_ge(CS.h, cs_p1)
                    scalar.wait_ge(CV.h, cv_p1)
                    scalar.wait_ge(CVS.h, cvs_conv)
                    first_bc[0] = False
                for q in range(3):
                    for mat in "gu":
                        emit_one_bc(scalar, t, mat, q)

        def emit_A_deq_q(t, mat, qs):
            @block.vector
            def _(vector):
                ibuf = {"g": ig, "u": iu}[mat][t % 2]
                wbuf = {"g": wg, "u": wu}[mat][t % 2]
                if qs[0] == 0:
                    vector.wait_ge(IQp[t % 2].h, iq_mark[t])
                    if t >= 2:
                        vector.wait_ge(CT.h, tu_mark[t - 2] if mat == "u" else tg_mark[t - 2])
                iv = ibuf[:].bitcast(I16).rearrange("p (K w h) -> p K w h", w=16, h=2)
                nv = s_nib[:].rearrange("p (K w e) -> p K w e", w=16, e=8)
                for q in qs:
                    vector.wait_ge({"g": BCg, "u": BCu}[mat][q % 3].h, bc_mark[(t, mat, q)])
                    for sh in range(4):
                        CV.inc(vector.tensor_scalar(
                            out=nv[:, 0:8, :, 2 * sh:2 * sh + 2],
                            in0=iv[:, 8 * q:8 * (q + 1), :, :],
                            scalar1=(4 * sh), scalar2=15,
                            op0=ALU.logical_shift_right, op1=ALU.bitwise_and))
                    vector.wait_ge(CV.h, CV.n)
                    CV.inc(vector.tensor_tensor(out=s_tmpb[:], in0=s_nib[:],
                                                in1=reps[("z", mat, q % 3)][:], op=ALU.subtract))
                    vector.wait_ge(CV.h, CV.n)
                    CV.inc(vector.tensor_tensor(out=wbuf[:, 1024 * q:1024 * (q + 1)], in0=s_tmpb[:],
                                                in1=reps[("s", mat, q % 3)][:], op=ALU.mult))
                    vd_mark_q[(t, mat, q)] = CV.n
                    vector.wait_ge(CV.h, CV.n)
                if qs[-1] == 3:
                    vd_mark[(t, mat)] = CV.n
                    exd_mark[t] = CV.n

        def emit_A_deq(t):
            for mat in "gu":
                emit_A_deq_q(t, mat, [0])

                @block.scalar
                def _(scalar):
                    emit_one_bc(scalar, t, mat, 3)

                emit_A_deq_q(t, mat, [1, 2, 3])

        def emit_A_mms(t):
            @block.tensor
            def _(tensor):
                tensor.wait_ge(CV.h, vd_mark[(t, "g")])
                if t == 0:
                    tensor.wait_ge(CV.h, cv_p1)
                if t >= 1:
                    tensor.wait_ge(CS.h, silu_mark[t - 1])
                for K in range(KT):
                    for c in range(4):
                        mm = tensor.matmul(out=pg[:, 512 * c:512 * (c + 1)],
                                           lhsT=wg[t % 2][:, 128 * K:128 * (K + 1)],
                                           rhs=s_xnT[:, MROWS * K + 512 * c: MROWS * K + 512 * (c + 1)],
                                           start=(K == 0), stop=(K == KT - 1))
                        if K == KT - 1 and c == 3:
                            CT.inc(mm)
                tg_mark[t] = CT.n
                tensor.wait_ge(CV.h, vd_mark[(t, "u")])
                if t >= 1:
                    tensor.wait_ge(CV.h, hmul_mark[t - 1])
                for K in range(KT):
                    for c in range(4):
                        mm = tensor.matmul(out=pu[:, 512 * c:512 * (c + 1)],
                                           lhsT=wu[t % 2][:, 128 * K:128 * (K + 1)],
                                           rhs=s_xnT[:, MROWS * K + 512 * c: MROWS * K + 512 * (c + 1)],
                                           start=(K == 0), stop=(K == KT - 1))
                        if K == KT - 1 and c == 3:
                            CT.inc(mm)
                tu_mark[t] = CT.n

        def emit_A_silu(t):
            @block.scalar
            def _(scalar):
                scalar.wait_ge(CT.h, tg_mark[t])
                if t >= 2:
                    scalar.wait_ge(HSp[t % 2].h, hspill_mark[t - 2])
                CS.inc(scalar.activation(out=hb[t % 2][:], in_=pg[:], func=AF.Silu))
                silu_mark[t] = CS.n

        def emit_A_hmul(t):
            @block.vector
            def _(vector):
                vector.wait_ge(CS.h, silu_mark[t])
                vector.wait_ge(CT.h, tu_mark[t])
                CV.inc(vector.tensor_tensor(out=hb[t % 2][:], in0=hb[t % 2][:], in1=pu[:], op=ALU.mult))
                hmul_mark[t] = CV.n
                vector.wait_ge(CV.h, CV.n)

        def emit_A_spill(t):
            @block.sync
            def _(sync):
                sync.wait_ge(CV.h, hmul_mark[t])
                HSp[t % 2].inc(sync.dma_start(
                    out=hscr[:, :, t, :].transpose([1, 0, 2]),
                    in_=hb[t % 2][:].rearrange("p (q m) -> p q m", q=4)), 16)
                hspill_mark[t] = HSp[t % 2].n

        emit_A_loads(0)
        emit_A_bc(0)
        for t in range(NNT):
            if t + 1 < NNT:
                emit_A_loads(t + 1)
            emit_A_deq(t)
            if t >= 1:
                emit_A_hmul(t - 1)
            if t + 1 < NNT:
                emit_A_bc(t + 1)
            emit_A_mms(t)
            emit_A_silu(t)
            if t >= 1:
                emit_A_spill(t - 1)
        emit_A_hmul(NNT - 1)
        emit_A_spill(NNT - 1)
        cv_A = CV.n
        hs_A = (HSp[0].n, HSp[1].n)

        esA.close()
        es_xnT.close()

        esB = ExitStack()
        mkB = esB.enter_context
        sbB = lambda n, s, d=F32: mkB(nc.sbuf_tensor(n, s, d))

        wd_slab = sbB("wd_slab", [128, NNT * 512], BF16)
        id4b = [sbB(f"id4b{b}", [128, BW * 64], I32) for b in range(2)]
        rps = [sbB(f"rps{b}", [128, BW * 512], BF16) for b in range(2)]
        rpz = [sbB(f"rpz{b}", [128, BW * 512], BF16) for b in range(2)]
        s_nib2 = sbB("s_nib2", [128, BW * 512], I16)
        HBW = 8
        NHBUF = 4
        hbl = [sbB(f"hbl{b}", [128, HBW * 512], BF16) for b in range(NHBUF)]
        xcb = [sbB(f"xcb{j}", [128, 512]) for j in range(8)]
        ccb = [sbB(f"ccb{j}", [128, 512]) for j in range(8)]
        xnc = [sbB(f"xnc{j}", [128, 512]) for j in range(2)]
        ob1 = [sbB(f"ob1{j}", [128, 512]) for j in range(2)]
        ob2 = [sbB(f"ob2{j}", [128, 512]) for j in range(2)]

        pd = [mkB(nc.psum_tensor(f"pd{i}", [128, 512], F32)) for i in range(8)]

        iq2 = {}
        bc2 = {}
        vd2b = {}
        hl2 = {}
        td2 = {}
        comb_v = {}
        xc_mark = {}

        def bof(i):
            return i // BW

        def emit_B_deq_batch(e, b):
            i0 = BSTARTS[b]
            bw = min(BW, NNT - i0)

            @block.scalar
            def _(scalar):
                if e == 0 and b == 0:
                    scalar.wait_ge(CV.h, cv_A)
                    scalar.wait_ge(HSp[0].h, hs_A[0])
                    scalar.wait_ge(HSp[1].h, hs_A[1])
                prevp = NB - 1 if (NB - 1) % 2 == b % 2 else NB - 2
                kb = (e, b - 2) if b >= 2 else (e - 1, prevp)
                if kb in vd2b:
                    scalar.wait_ge(CV.h, vd2b[kb])
                IQp[b % 2].inc(scalar.dma_start(
                    out=id4b[b % 2][:, 0:bw * 64].rearrange("p (I w) -> p I w", w=64),
                    in_=dqw[128 * i0:128 * (i0 + bw), 64 * e:64 * (e + 1)].rearrange(
                        "(I p) w -> p I w", p=128)), 16)
                iq2[(e, b)] = IQp[b % 2].n
                BCd[b % 2].inc(scalar.dma_start(
                    out=rps[b % 2][:, 0:bw * 512].rearrange("p (I n) -> p I n", n=512),
                    in_=ssc_d[e, i0:i0 + bw, :].partition_broadcast(128)), 16)
                BCd[b % 2].inc(scalar.dma_start(
                    out=rpz[b % 2][:, 0:bw * 512].rearrange("p (I n) -> p I n", n=512),
                    in_=zsc_d[e, i0:i0 + bw, :].partition_broadcast(128)), 16)
                bc2[(e, b)] = BCd[b % 2].n

            @block.vector
            def _(vector):
                vector.wait_ge(IQp[b % 2].h, iq2[(e, b)])
                vector.wait_ge(BCd[b % 2].h, bc2[(e, b)])
                if e >= 1:
                    vector.wait_ge(CT.h, td2[(e - 1, 3, i0 + bw - 1)])
                iv = id4b[b % 2][:].bitcast(I16).rearrange("p (I w h) -> p I w h", w=64, h=2)
                nv = s_nib2[:].rearrange("p (I w e) -> p I w e", w=64, e=8)
                for sh in range(4):
                    CV.inc(vector.tensor_scalar(
                        out=nv[:, 0:bw, :, 2 * sh:2 * sh + 2],
                        in0=iv[:, 0:bw, :, :],
                        scalar1=(4 * sh), scalar2=15,
                        op0=ALU.logical_shift_right, op1=ALU.bitwise_and))
                vector.wait_ge(CV.h, CV.n)
                tb2 = s_nib2[:].bitcast(BF16)
                CV.inc(vector.tensor_tensor(out=tb2[:, 0:bw * 512], in0=s_nib2[:, 0:bw * 512],
                                            in1=rpz[b % 2][:, 0:bw * 512], op=ALU.subtract))
                vector.wait_ge(CV.h, CV.n)
                CV.inc(vector.tensor_tensor(out=wd_slab[:, 512 * i0:512 * (i0 + bw)],
                                            in0=tb2[:, 0:bw * 512],
                                            in1=rps[b % 2][:, 0:bw * 512], op=ALU.mult))
                vd2b[(e, b)] = CV.n
                vector.wait_ge(CV.h, CV.n)

        NHB = (NNT + HBW - 1) // HBW

        def emit_B_quad(e, quad):
            @block.gpsimd
            def _(gpsimd):
                pkey = (e, quad - 2) if quad >= 2 else (e - 1, quad + 2)
                if pkey in comb_v:
                    gpsimd.wait_ge(CV.h, comb_v[pkey])
                if e == 0 and quad == 0:
                    gpsimd.wait_ge(CVS.h, cvs_conv)
                    gpsimd.wait_ge(CV.h, cv_A)
                qp = 4 * (quad % 2)
                for jj in range(4):
                    mt = 4 * quad + jj
                    XG.inc(gpsimd.dma_start(
                        out=xcb[qp + jj][:],
                        in_=x_in[128 * mt:128 * (mt + 1), 512 * e:512 * (e + 1)]), 16)
                    XG.inc(gpsimd.dma_start(
                        out=ccb[qp + jj][:],
                        in_=cscr[128 * mt:128 * (mt + 1), 512 * e:512 * (e + 1)]), 16)
                xc_mark[(e, quad)] = XG.n

            pbase = 4 * (quad % 2)
            for hB in range(NHB):
                i0h = HBW * hB
                hw = min(HBW, NNT - i0h)

                @block.sync
                def _(sync):
                    if hB >= NHBUF:
                        pb2 = HBW * (hB - NHBUF)
                        sync.wait_ge(CT.h, td2[(e, quad, pb2 + min(HBW, NNT - pb2) - 1)])
                    else:
                        qk = (e, quad - 1) if quad >= 1 else (e - 1, 3)
                        prev_b = NHB - 1 - ((NHB - 1 - hB) % NHBUF)
                        if prev_b > NHB - 1 - NHBUF:
                            pass
                        prev_b = max(b for b in range(NB + NHB) if b < NHB and b % NHBUF == hB % NHBUF)
                        if (qk[0], qk[1], NNT - 1) in td2:
                            last_i = min(HBW * prev_b + HBW, NNT) - 1
                            sync.wait_ge(CT.h, td2[(qk[0], qk[1], last_i)])
                    HLp[hB % NHBUF].inc(sync.dma_start(
                        out=hbl[hB % NHBUF][:, 0:hw * 512].rearrange("p (I m) -> p I m", m=512),
                        in_=hscr[quad, :, i0h:i0h + hw, :]), 16)
                    hl2[(e, quad, hB)] = HLp[hB % NHBUF].n

                @block.tensor
                def _(tensor):
                    tensor.wait_ge(HLp[hB % NHBUF].h, hl2[(e, quad, hB)])
                    for i in range(i0h, i0h + hw):
                        if i == 0 or bof(i) != bof(i - 1):
                            tensor.wait_ge(CV.h, vd2b[(e, bof(i))])
                        if i == 0:
                            if e == 0 and quad == 0:
                                tensor.wait_ge(CV.h, cv_A)
                            key = (e, quad - 2) if quad >= 2 else (e - 1, quad + 2)
                            if key in comb_v:
                                tensor.wait_ge(CV.h, comb_v[key])
                        for jj in range(4):
                            mm = tensor.matmul(out=pd[pbase + jj][:],
                                               lhsT=hbl[hB % NHBUF][:, 512 * (i - i0h) + 128 * jj:
                                                                    512 * (i - i0h) + 128 * (jj + 1)],
                                               rhs=wd_slab[:, 512 * i:512 * (i + 1)],
                                               start=(i == 0), stop=(i == NNT - 1))
                            if jj == 3:
                                CT.inc(mm)
                        td2[(e, quad, i)] = CT.n

        def emit_B_combine(e, quad):
            qp = 4 * (quad % 2)
            for jj in range(4):
                mt = 4 * quad + jj

                @block.vector
                def _(vector):
                    vector.wait_ge(CT.h, td2[(e, quad, NNT - 1)])
                    vector.wait_ge(XG.h, xc_mark[(e, quad)])
                    ok = (e, quad, jj - 2, "so") if jj >= 2 else (
                        (e, quad - 1, jj + 2, "so") if quad >= 1 else (e - 1, 3, jj + 2, "so"))
                    if ok in marks:
                        vector.wait_ge(SOp[jj % 2].h, marks[ok])
                    CV.inc(vector.tensor_scalar(out=xnc[jj % 2][:], in0=xcb[qp + jj][:],
                                                scalar1=s_rstd[:, mt:mt + 1],
                                                scalar2=s_nbias[:, mt:mt + 1],
                                                op0=ALU.mult, op1=ALU.add))
                    CV.inc(vector.tensor_tensor(out=ob1[jj % 2][:], in0=pd[qp + jj][:],
                                                in1=xcb[qp + jj][:], op=ALU.add))
                    vector.wait_ge(CV.h, CV.n)
                    CV.inc(vector.scalar_tensor_tensor(out=ob2[jj % 2][:], in0=xnc[jj % 2][:],
                                                       scalar=s_dd[:], in1=ob1[jj % 2][:],
                                                       op0=ALU.mult, op1=ALU.add))
                    vector.wait_ge(CV.h, CV.n)
                    CV.inc(vector.tensor_tensor(out=ob1[jj % 2][:], in0=ob2[jj % 2][:],
                                                in1=ccb[qp + jj][:], op=ALU.add))
                    marks[(e, quad, jj)] = CV.n
                    vector.wait_ge(CV.h, CV.n)
                    if jj == 3:
                        comb_v[(e, quad)] = CV.n

                @block.gpsimd
                def _(gpsimd):
                    gpsimd.wait_ge(CV.h, marks[(e, quad, jj)])
                    SOp[jj % 2].inc(gpsimd.dma_start(
                        out=out[128 * mt:128 * (mt + 1), 512 * e:512 * (e + 1)],
                        in_=ob1[jj % 2][:]), 16)
                    marks[(e, quad, jj, "so")] = SOp[jj % 2].n

        for b in range(NB):
            emit_B_deq_batch(0, b)
        for e in range(8):
            for quad in range(4):
                emit_B_quad(e, quad)
                if quad == 3 and e + 1 < 8:
                    for b in range(0, 9):
                        emit_B_deq_batch(e + 1, b)
                emit_B_combine(e, quad)
            if e + 1 < 8:
                for b in range(9, NB):
                    emit_B_deq_batch(e + 1, b)

        @block.gpsimd
        def _(gpsimd):
            gpsimd.wait_ge(SOp[0].h, SOp[0].n)
            gpsimd.wait_ge(SOp[1].h, SOp[1].n)

        esB.close()

    return nc


_BUILD_CACHE = {}


def _get_nc():
    if "nc" not in _BUILD_CACHE:
        _BUILD_CACHE["nc"] = build()
    return _BUILD_CACHE["nc"]


def kernel(**inputs):
    x = np.ascontiguousarray(np.asarray(inputs["L0_attn_u"], dtype=np.float32).reshape(ROWS, DM))
    consts = host_consts()
    com = {
        "lam": np.ascontiguousarray(np.asarray(inputs["L0_attn_Lambda_ri"], np.float32)),
        "p_ri": np.ascontiguousarray(np.asarray(inputs["L0_attn_p_ri"], np.float32)),
        "q_ri": np.ascontiguousarray(np.asarray(inputs["L0_attn_q_ri"], np.float32)),
        "vc": np.ascontiguousarray(np.asarray(inputs["L0_attn_Vc_ri"], np.float32)),
        "ct_in": np.ascontiguousarray(np.asarray(inputs["L0_attn_Ct_ri"], np.float32)),
        "bp": np.ascontiguousarray(np.asarray(inputs["L0_attn_Bp"], np.float32)),
        "lsr": np.ascontiguousarray(np.broadcast_to(
            np.asarray(inputs["L0_attn_log_step"], np.float32).reshape(1, 1), (64, 1))),
        "dd": np.ascontiguousarray(np.broadcast_to(
            np.asarray(inputs["L0_attn_D"], np.float32).reshape(1, 1), (128, 1))),
        "gqw": np.ascontiguousarray(np.asarray(inputs["L0_ffn_gate_qweight"], np.int32)),
        "gsc": np.ascontiguousarray(np.asarray(inputs["L0_ffn_gate_scales"], np.float32)),
        "gqz": np.ascontiguousarray(np.asarray(inputs["L0_ffn_gate_qzeros"], np.int32)),
        "uqw": np.ascontiguousarray(np.asarray(inputs["L0_ffn_up_qweight"], np.int32)),
        "usc": np.ascontiguousarray(np.asarray(inputs["L0_ffn_up_scales"], np.float32)),
        "uqz": np.ascontiguousarray(np.asarray(inputs["L0_ffn_up_qzeros"], np.int32)),
        "dqw": np.ascontiguousarray(np.asarray(inputs["L0_ffn_down_qweight"], np.int32)),
        "dsc": np.ascontiguousarray(np.asarray(inputs["L0_ffn_down_scales"], np.float32)),
        "dqz": np.ascontiguousarray(np.asarray(inputs["L0_ffn_down_qzeros"], np.int32)),
    }
    com.update(consts)
    in_maps = []
    for c in range(NCORE):
        m = dict(com)
        m["x"] = x[MROWS * c:MROWS * (c + 1), :]
        in_maps.append(m)
    nc = _get_nc()
    res = run_bass_kernel_spmd(nc, in_maps, core_ids=list(range(NCORE)))
    parts = [np.asarray(r["out"]) for r in res.results]
    return np.concatenate(parts, axis=0).reshape(B, S, DM)
